# version 02
# baseline (speedup 1.0000x reference)
import numpy as np
import ml_dtypes
import concourse.bass as bass
import concourse.mybir as mybir
from concourse.bass_utils import run_bass_kernel_spmd
from contextlib import ExitStack

F32 = mybir.dt.float32
BF16 = mybir.dt.bfloat16
U8 = mybir.dt.uint8
AF = mybir.ActivationFunctionType
ALU = mybir.AluOpType
AX = mybir.AxisListType

MAGIC = 12582912.0
QMAX = 15.0
SHRINKS = [float(np.float32(s)) for s in np.linspace(0.4, 1.0, 16)]
ATT_SCALE = float(1.0 / np.sqrt(np.float32(128.0)))
EPS = 1e-5

D = 2048
S = 2048
NCORE = 8

DBG_STAGE = "full"
DBG_LN_TILES = 32
DBG_LN_PARTS = "all"
DBG_COMB = "full"



class Tok:
    __slots__ = ("sem", "val", "eng", "dma")
    def __init__(self, sem, val, eng, dma):
        self.sem, self.val, self.eng, self.dma = sem, val, eng, dma


class Buf:
    __slots__ = ("ap", "w", "r")
    def __init__(self, ap):
        self.ap = ap
        self.w = None
        self.r = []


class Ctx:
    def __init__(self, nc, stack):
        self.nc = nc
        self.stack = stack
        self.sems = {}
        self.counts = {}
        self.waited = {}
        self.phase = "p"

    def _sem(self, name):
        if name not in self.sems:
            self.sems[name] = self.stack.enter_context(self.nc.semaphore(name))
            self.counts[name] = 0
        return self.sems[name]

    def await_tok(self, eng, eng_name, tok):
        if tok is None:
            return
        if isinstance(tok, (list, tuple)):
            for t in tok:
                self.await_tok(eng, eng_name, t)
            return
        if (not tok.dma) and tok.eng == eng_name:
            return
        key = (eng_name, tok.sem)
        if self.waited.get(key, -1) >= tok.val:
            return
        eng.wait_ge(self.sems[tok.sem], tok.val)
        self.waited[key] = tok.val

    def op(self, eng_name, fn, reads=(), writes=(), extra=(), dma=False, sem=None):
        eng = getattr(self.nc, eng_name)
        if eng_name in ("vector", "scalar", "gpsimd") and not dma:
            hist = getattr(self, "_last_writes", {}).get(eng_name, [])
            recent = set().union(*hist) if hist else set()
            if any(id(b) in recent for b in list(reads) + list(writes)):
                eng.drain()
                hist = []
        for b in reads:
            self.await_tok(eng, eng_name, b.w)
        for b in writes:
            self.await_tok(eng, eng_name, b.w)
            self.await_tok(eng, eng_name, b.r)
        for t in extra:
            self.await_tok(eng, eng_name, t)
        inst = fn(eng)
        if not hasattr(self, "_last_writes"):
            self._last_writes = {}
        if eng_name in ("vector", "scalar", "gpsimd") and not dma:
            hist = hist + [frozenset(id(b) for b in writes)]
            self._last_writes[eng_name] = hist[-2:]
        sname = sem or (self.phase + "_" + eng_name)
        h = self._sem(sname)
        k = 16 if dma else 1
        inst.then_inc(h, k)
        self.counts[sname] += k
        tok = Tok(sname, self.counts[sname], eng_name, dma)
        for b in reads:
            b.r.append(tok)
        for b in writes:
            b.w = tok
            b.r = []
        return tok

    _uid = [0]

    def _uname(self, name):
        self._uid[0] += 1
        return f"{name}_{self._uid[0]}"

    def sb(self, name, shape, dtype=F32):
        return Buf(self.stack.enter_context(
            self.nc.sbuf_tensor(self._uname(name), shape, dtype)).ap())

    def ps(self, name, shape, dtype=F32):
        return Buf(self.stack.enter_context(
            self.nc.psum_tensor(self._uname(name), shape, dtype)).ap())

    def scope(self):
        import contextlib

        @contextlib.contextmanager
        def _s():
            old = self.stack
            with ExitStack() as st:
                self.stack = st
                try:
                    yield
                finally:
                    self.barrier_all()
                    self.stack = old
        return _s()

    def barrier_all(self):
        for en in ("sync", "vector", "scalar", "tensor", "gpsimd"):
            eng = getattr(self.nc, en)
            for sname, cnt in self.counts.items():
                if cnt <= 0:
                    continue
                key = (en, sname)
                if self.waited.get(key, -1) >= cnt:
                    continue
                eng.wait_ge(self.sems[sname], cnt)
                self.waited[key] = cnt

    def dram(self, name, shape, dtype, kind="Internal", addr_space=None):
        if addr_space is None:
            t = self.nc.dram_tensor(name, shape, dtype, kind=kind)
        else:
            t = self.nc.dram_tensor(name, shape, dtype, kind=kind, addr_space=addr_space)
        return Buf(t.ap())


def emit_recip(ctx, out, in_, t1, t2):
    nc = ctx.nc
    ctx.op("scalar", lambda e: e.activation(out=t1.ap, in_=in_.ap, func=AF.Ln),
           reads=[in_], writes=[t1])
    ctx.op("scalar", lambda e: e.activation(out=t1.ap, in_=t1.ap, func=AF.Exp, scale=-1.0),
           reads=[t1], writes=[t1])
    ctx.op("vector", lambda e: e.tensor_tensor(out=t2.ap, in0=in_.ap, in1=t1.ap, op=ALU.mult),
           reads=[in_, t1], writes=[t2])
    ctx.op("vector", lambda e: e.tensor_scalar(out=t2.ap, in0=t2.ap, scalar1=-1.0, scalar2=2.0,
                                               op0=ALU.mult, op1=ALU.add),
           reads=[t2], writes=[t2])
    ctx.op("vector", lambda e: e.tensor_tensor(out=out.ap, in0=t1.ap, in1=t2.ap, op=ALU.mult),
           reads=[t1, t2], writes=[out])


def emit_rsqrt(ctx, out, in_, t1, t2):
    nc = ctx.nc
    ctx.op("scalar", lambda e: e.activation(out=t1.ap, in_=in_.ap, func=AF.Ln),
           reads=[in_], writes=[t1])
    ctx.op("scalar", lambda e: e.activation(out=t1.ap, in_=t1.ap, func=AF.Exp, scale=-0.5),
           reads=[t1], writes=[t1])
    ctx.op("vector", lambda e: e.tensor_tensor(out=t2.ap, in0=t1.ap, in1=t1.ap, op=ALU.mult),
           reads=[t1], writes=[t2])
    ctx.op("vector", lambda e: e.tensor_tensor(out=t2.ap, in0=in_.ap, in1=t2.ap, op=ALU.mult),
           reads=[in_, t2], writes=[t2])
    ctx.op("vector", lambda e: e.tensor_scalar(out=t2.ap, in0=t2.ap, scalar1=-0.5, scalar2=1.5,
                                               op0=ALU.mult, op1=ALU.add),
           reads=[t2], writes=[t2])
    ctx.op("vector", lambda e: e.tensor_tensor(out=out.ap, in0=t1.ap, in1=t2.ap, op=ALU.mult),
           reads=[t1, t2], writes=[out])


def view_blk(ap2d, w=64):
    return ap2d.rearrange("p (b w) -> p b w", w=w)


def bcast_blk(ap2d, nblk, w=64):
    return ap2d.rearrange("p (b o) -> p b o", o=1).broadcast_to([128, nblk, w])



def load_const(ctx, buf_dram, name, shape, dtype):
    b = ctx.sb(name, shape, dtype)
    ctx.op("sync", lambda e: e.dma_start(out=b.ap, in_=buf_dram.ap), reads=[buf_dram],
           writes=[b], dma=True)
    return b


def build_wT(ctx, w_nat, dest, ident, wnatb, tp_ps, oc_list, dest_col0=None,
             evac_eng="scalar"):
    wnv = w_nat.ap.rearrange("(t p) d -> p t d", p=128)
    for i, oc in enumerate(oc_list):
        sl = i % 2
        col0 = oc * 128 if dest_col0 is None else dest_col0 + i * 128
        ctx.op("gpsimd", lambda e, oc=oc, sl=sl: e.dma_start(
            out=wnatb[sl].ap, in_=wnv[:, oc, :]),
            reads=[w_nat], writes=[wnatb[sl]], dma=True)
        for dc in range(16):
            tsl = dc % 2
            ctx.op("tensor",
                   lambda e, sl=sl, dc=dc, tsl=tsl: e.transpose(
                       out=tp_ps[tsl].ap[:, 0:128],
                       in_=wnatb[sl].ap[:, dc * 128:(dc + 1) * 128], identity=ident.ap),
                   reads=[wnatb[sl], ident], writes=[tp_ps[tsl]])
            ctx.op(evac_eng,
                   lambda e, col0=col0, dc=dc, tsl=tsl: (
                       e.copy(out=dest.ap[:, dc, col0:col0 + 128],
                              in_=tp_ps[tsl].ap[:, 0:128])
                       if evac_eng == "scalar" else
                       e.tensor_copy(dest.ap[:, dc, col0:col0 + 128],
                                     tp_ps[tsl].ap[:, 0:128])),
                   reads=[tp_ps[tsl]], writes=[dest])



def emit_quant(ctx, io, ident):
    ctx.phase = "q"
    with ctx.scope():
        _emit_quant_body(ctx, io, ident)


def _emit_quant_body(ctx, io, ident):
    nc = ctx.nc
    wbuf = ctx.sb("q_w", [128, 8192])
    tbuf = ctx.sb("q_t", [128, 8192])
    qbuf = ctx.sb("q_q", [128, 8192])
    stage = ctx.sb("q_stage", [128, 16384], BF16)
    amax = ctx.sb("q_amax", [128, 128])
    base = ctx.sb("q_base", [128, 128])
    sc = ctx.sb("q_sc", [128, 128])
    sc2 = ctx.sb("q_sc2", [128, 128])
    rs = ctx.sb("q_rs", [128, 128])
    errsl = ctx.sb("q_err", [128, 128])
    b_err = ctx.sb("q_berr", [128, 128])
    b_rs = ctx.sb("q_brs", [128, 128])
    b_sc = ctx.sb("q_bsc", [128, 128])
    mask8 = ctx.sb("q_mask", [128, 128], U8)
    qh1 = ctx.sb("q_h1", [128, 128])
    qh2 = ctx.sb("q_h2", [128, 128])
    dqs = [ctx.sb(f"q_dqs{i}", [128, 128], BF16) for i in range(2)]
    tp_ps = [ctx.ps(f"q_tp{i}", [128, 1024], BF16) for i in range(2)]

    gath_toks = []
    for wname, wsrc, cc_in, ntile_h, icols in (
        ("fc", io["wfc_sl"], io["cc_in_fc"], 32, 256),
        ("pj", io["wpj_sl"], io["cc_in_pj"], 8, 1024),
    ):
        nhalf = 2
        o_total = ntile_h * nhalf * 128
        njj = icols // 128
        ncol = ntile_h * icols
        nblk = ncol // 64
        stage_v = stage.ap.rearrange("p (jj o) -> p jj o", o=o_total)
        for g in range(nhalf):
            src = wsrc.ap.rearrange("(t p) i -> p t i", p=128)[:, g * ntile_h:(g + 1) * ntile_h, :]
            ctx.op("sync", lambda e: e.dma_start(
                out=wbuf.ap.rearrange("p (t i) -> p t i", i=icols)[:, 0:ntile_h, :], in_=src),
                reads=[wsrc], writes=[wbuf], dma=True)
            half = wbuf.ap[:, 0:ncol]
            ctx.op("vector", lambda e: e.tensor_reduce(
                out=amax.ap, in_=view_blk(half), axis=AX.X, op=ALU.max,
                apply_absolute_value=True), reads=[wbuf], writes=[amax])
            ctx.op("vector", lambda e: e.tensor_scalar(
                out=base.ap, in0=amax.ap, scalar1=1e-8, scalar2=1.0 / 15.0,
                op0=ALU.max, op1=ALU.mult), reads=[amax], writes=[base])
            for i, s_i in enumerate(SHRINKS):
                ctx.op("vector", lambda e, s_i=s_i: e.tensor_scalar(
                    out=sc.ap, in0=base.ap, scalar1=s_i, scalar2=None, op0=ALU.mult),
                    reads=[base], writes=[sc])
                emit_recip(ctx, rs, sc, qh1, qh2)
                ctx.op("vector", lambda e: e.tensor_tensor(
                    out=sc2.ap, in0=sc.ap, in1=sc.ap, op=ALU.mult), reads=[sc], writes=[sc2])
                ctx.op("vector", lambda e: e.tensor_tensor(
                    out=view_blk(tbuf.ap[:, 0:ncol]), in0=view_blk(half),
                    in1=bcast_blk(rs.ap, nblk), op=ALU.mult),
                    reads=[wbuf, rs], writes=[tbuf])
                ctx.op("vector", lambda e: e.tensor_scalar(
                    out=qbuf.ap[:, 0:ncol], in0=tbuf.ap[:, 0:ncol], scalar1=MAGIC,
                    scalar2=MAGIC + QMAX, op0=ALU.add, op1=ALU.min),
                    reads=[tbuf], writes=[qbuf])
                ctx.op("vector", lambda e: e.tensor_scalar(
                    out=qbuf.ap[:, 0:ncol], in0=qbuf.ap[:, 0:ncol], scalar1=MAGIC - QMAX,
                    scalar2=MAGIC, op0=ALU.max, op1=ALU.subtract),
                    reads=[qbuf], writes=[qbuf])
                ctx.op("gpsimd", lambda e: e.tensor_sub(
                    qbuf.ap[:, 0:ncol], qbuf.ap[:, 0:ncol], tbuf.ap[:, 0:ncol]),
                    reads=[qbuf, tbuf], writes=[qbuf])
                ctx.op("scalar", lambda e: e.activation(
                    out=qbuf.ap[:, 0:ncol], in_=qbuf.ap[:, 0:ncol], func=AF.Square),
                    reads=[qbuf], writes=[qbuf])
                ctx.op("vector", lambda e: e.tensor_reduce(
                    out=errsl.ap, in_=view_blk(qbuf.ap[:, 0:ncol]), axis=AX.X, op=ALU.add),
                    reads=[qbuf], writes=[errsl])
                ctx.op("vector", lambda e: e.tensor_tensor(
                    out=errsl.ap, in0=errsl.ap, in1=sc2.ap, op=ALU.mult),
                    reads=[errsl, sc2], writes=[errsl])
                if i == 0:
                    ctx.op("vector", lambda e: e.tensor_copy(b_err.ap, errsl.ap),
                           reads=[errsl], writes=[b_err])
                    ctx.op("vector", lambda e: e.tensor_copy(b_rs.ap, rs.ap),
                           reads=[rs], writes=[b_rs])
                    ctx.op("vector", lambda e: e.tensor_copy(b_sc.ap, sc.ap),
                           reads=[sc], writes=[b_sc])
                else:
                    ctx.op("vector", lambda e: e.tensor_tensor(
                        out=mask8.ap, in0=errsl.ap, in1=b_err.ap, op=ALU.is_lt),
                        reads=[errsl, b_err], writes=[mask8])
                    ctx.op("vector", lambda e: e.select(
                        out=b_err.ap, mask=mask8.ap, on_true=errsl.ap, on_false=b_err.ap),
                        reads=[mask8, errsl, b_err], writes=[b_err])
                    ctx.op("vector", lambda e: e.select(
                        out=b_rs.ap, mask=mask8.ap, on_true=rs.ap, on_false=b_rs.ap),
                        reads=[mask8, rs, b_rs], writes=[b_rs])
                    ctx.op("vector", lambda e: e.select(
                        out=b_sc.ap, mask=mask8.ap, on_true=sc.ap, on_false=b_sc.ap),
                        reads=[mask8, sc, b_sc], writes=[b_sc])
            ctx.op("vector", lambda e: e.tensor_tensor(
                out=view_blk(tbuf.ap[:, 0:ncol]), in0=view_blk(half),
                in1=bcast_blk(b_rs.ap, nblk), op=ALU.mult), reads=[wbuf, b_rs], writes=[tbuf])
            ctx.op("vector", lambda e: e.tensor_scalar(
                out=qbuf.ap[:, 0:ncol], in0=tbuf.ap[:, 0:ncol], scalar1=MAGIC,
                scalar2=MAGIC + QMAX, op0=ALU.add, op1=ALU.min), reads=[tbuf], writes=[qbuf])
            ctx.op("vector", lambda e: e.tensor_scalar(
                out=qbuf.ap[:, 0:ncol], in0=qbuf.ap[:, 0:ncol], scalar1=MAGIC - QMAX,
                scalar2=MAGIC, op0=ALU.max, op1=ALU.subtract), reads=[qbuf], writes=[qbuf])
            for t in range(ntile_h):
                for jj in range(njj):
                    c0 = t * icols + jj * 128
                    slot = (t * njj + jj) % 2
                    ctx.op("vector", lambda e, c0=c0, slot=slot: e.tensor_tensor(
                        out=view_blk(dqs[slot].ap), in0=view_blk(qbuf.ap[:, c0:c0 + 128]),
                        in1=bcast_blk(b_sc.ap[:, c0 // 64:c0 // 64 + 2], 2), op=ALU.mult),
                        reads=[qbuf, b_sc], writes=[dqs[slot]])
                    ctx.op("tensor", lambda e, slot=slot: e.transpose(
                        out=tp_ps[slot].ap[:, 0:128], in_=dqs[slot].ap, identity=ident.ap),
                        reads=[dqs[slot], ident], writes=[tp_ps[slot]])
                    dst = stage_v[:, jj, g * ntile_h * 128 + t * 128:
                                  g * ntile_h * 128 + (t + 1) * 128]
                    ctx.op("scalar", lambda e, slot=slot, dst=dst: e.copy(
                        out=dst, in_=tp_ps[slot].ap[:, 0:128]),
                        reads=[tp_ps[slot]], writes=[stage])
        for jj in range(njj):
            ctx.op("sync", lambda e, jj=jj: e.dma_start(
                out=cc_in.ap.rearrange("(jj p) o -> p jj o", p=128)[:, jj, :],
                in_=stage_v[:, jj, 0:o_total]),
                reads=[stage], writes=[cc_in], dma=True)
    rg = [list(range(NCORE))]
    for cc_in, cc_out in ((io["cc_in_fc"], io["cc_out_fc"]), (io["cc_in_pj"], io["cc_out_pj"])):
        ctx.op("gpsimd", lambda e, a=cc_in, b=cc_out: e.collective_compute(
            "AllGather", ALU.bypass, replica_groups=rg, ins=[a.ap], outs=[b.ap]),
            reads=[cc_in], writes=[cc_out])



def emit_ln(ctx, io, ident):
    ctx.phase = "ln"
    nc = ctx.nc
    with ctx.scope():
        _emit_ln_body(ctx, io, ident)


def _emit_ln_body(ctx, io, ident):
    nc = ctx.nc
    xt = [ctx.sb(f"ln_x{i}", [128, D]) for i in range(2)]
    s1 = ctx.sb("ln_s1", [128, 1])
    s2 = ctx.sb("ln_s2", [128, 1])
    mean = ctx.sb("ln_mean", [128, 1])
    msq = ctx.sb("ln_msq", [128, 1])
    c1 = ctx.sb("ln_c1", [128, 1])
    h1 = ctx.sb("ln_h1", [128, 1])
    h2 = ctx.sb("ln_h2", [128, 1])
    rstd = ctx.sb("ln_rstd", [128, 1])
    negmb = ctx.sb("ln_negmb", [128, 1])
    xn_tm = ctx.sb("ln_xntm", [128, D], BF16)
    xstage = [ctx.sb(f"ln_stg{i}", [128, 16, 512], BF16) for i in range(2)]
    tstage = [ctx.sb(f"ln_tstg{i}", [128, 16, 128], BF16) for i in range(2)]
    xp_ps = [ctx.ps(f"ln_ps{i}", [128, 1024], BF16) for i in range(2)]

    xnd = io["xn_f"].ap.rearrange("(dc p) t -> p dc t", p=128)
    xtd = io["xn_t"].ap.rearrange("tt p dc t -> p tt dc t")
    x_t = io["x"].ap.rearrange("(t p) e -> p t e", p=128)

    for ti in range(DBG_LN_TILES):
        sl = ti % 2
        grp = ti // 4
        gsl = grp % 2
        ctx.op("sync", lambda e, sl=sl, ti=ti: e.dma_start(out=xt[sl].ap, in_=x_t[:, ti, :]),
               reads=[io["x"]], writes=[xt[sl]], dma=True)
        ctx.op("vector", lambda e, sl=sl: e.reduce_sum(
            out=s1.ap, in_=xt[sl].ap, axis=AX.X), reads=[xt[sl]], writes=[s1])
        ctx.op("scalar", lambda e, sl=sl: e.activation(
            out=xn_tm.ap, in_=xt[sl].ap, func=AF.Square, accum_out=s2.ap),
            reads=[xt[sl]], writes=[xn_tm, s2])
        ctx.op("vector", lambda e: e.tensor_scalar(
            out=mean.ap, in0=s1.ap, scalar1=1.0 / D, scalar2=None, op0=ALU.mult),
            reads=[s1], writes=[mean])
        ctx.op("vector", lambda e: e.tensor_tensor(
            out=msq.ap, in0=mean.ap, in1=mean.ap, op=ALU.mult), reads=[mean], writes=[msq])
        ctx.op("vector", lambda e: e.tensor_scalar(
            out=c1.ap, in0=s2.ap, scalar1=1.0 / D, scalar2=None, op0=ALU.mult),
            reads=[s2], writes=[c1])
        ctx.op("vector", lambda e: e.tensor_tensor(
            out=c1.ap, in0=c1.ap, in1=msq.ap, op=ALU.subtract), reads=[c1, msq], writes=[c1])
        ctx.op("vector", lambda e: e.tensor_scalar(
            out=c1.ap, in0=c1.ap, scalar1=EPS, scalar2=None, op0=ALU.add),
            reads=[c1], writes=[c1])
        emit_rsqrt(ctx, rstd, c1, h1, h2)
        ctx.op("vector", lambda e: e.tensor_tensor(
            out=negmb.ap, in0=mean.ap, in1=rstd.ap, op=ALU.mult), reads=[mean, rstd], writes=[negmb])
        ctx.op("vector", lambda e: e.tensor_scalar(
            out=negmb.ap, in0=negmb.ap, scalar1=-1.0, scalar2=None, op0=ALU.mult),
            reads=[negmb], writes=[negmb])
        ctx.op("scalar", lambda e, sl=sl: e.activation(
            out=xn_tm.ap, in_=xt[sl].ap, func=AF.Identity, bias=negmb.ap, scale=rstd.ap),
            reads=[xt[sl], negmb, rstd], writes=[xn_tm])
        if DBG_LN_PARTS == "math":
            continue
        for dc in range(16):
            tsl = dc % 2
            ctx.op("tensor", lambda e, dc=dc, tsl=tsl: e.transpose(
                out=xp_ps[tsl].ap[:, 0:128], in_=xn_tm.ap[:, dc * 128:(dc + 1) * 128],
                identity=ident.ap), reads=[xn_tm, ident], writes=[xp_ps[tsl]])
            tv = ctx.op("vector", lambda e, dc=dc, tsl=tsl, gsl=gsl, ti=ti: e.tensor_copy(
                xstage[gsl].ap[:, dc, (ti % 4) * 128:(ti % 4 + 1) * 128],
                xp_ps[tsl].ap[:, 0:128]), reads=[xp_ps[tsl]], writes=[xstage[gsl]])
            if DBG_LN_PARTS not in ("notstage",):
                ctx.op("scalar", lambda e, dc=dc, tsl=tsl, sl=sl: e.copy(
                    out=tstage[sl].ap[:, dc, :], in_=xp_ps[tsl].ap[:, 0:128]),
                    reads=[xp_ps[tsl]], writes=[tstage[sl]], extra=[tv])
        if DBG_LN_PARTS not in ("notstage", "nodma"):
            ctx.op("sync", lambda e, ti=ti, sl=sl: e.dma_start(
                out=xtd[:, ti, :, :], in_=tstage[sl].ap),
                reads=[tstage[sl]], writes=[io["xn_t"]], dma=True)
        if ti % 4 == 3:
            ctx.op("sync", lambda e, grp=grp, gsl=gsl: e.dma_start(
                out=xnd[:, :, grp * 512:(grp + 1) * 512], in_=xstage[gsl].ap),
                reads=[xstage[gsl]], writes=[io["xn_f"]], dma=True)


def emit_attn_seq(ctx, io, s, ident, mk_sb, gw_sb, gb_sb):
    nc = ctx.nc
    ctx.phase = f"a{s}"
    with ctx.scope():
        K_res = ctx.sb(f"at_K{s}", [128, 16, S], BF16)
        xnd = io["xn_f"].ap.rearrange("(dc p) t -> p dc t", p=128)
        xtd = io["xn_t"].ap.rearrange("tt p dc t -> p tt dc t")
        qd = io["q_d"].ap.rearrange("s qi p dc q -> p s qi dc q")

        with ctx.scope():
            xnf = ctx.sb(f"at_xnf{s}", [128, 16, S], BF16)
            ctx.op("sync", lambda e: e.dma_start(out=xnf.ap, in_=xnd[:, :, s * S:(s + 1) * S]),
                   reads=[io["xn_f"]], writes=[xnf], dma=True)
            wsl = [ctx.sb(f"at_w{s}{i}", [128, 16, 128], BF16) for i in range(2)]
            wnatb = [ctx.sb(f"at_wnb{s}{i}", [128, 2048], BF16) for i in range(2)]
            qstage = [ctx.sb(f"at_qs{s}{i}", [128, 512], BF16) for i in range(2)]
            tp_ps = [ctx.ps(f"at_tp{s}{i}", [128, 1024], BF16) for i in range(2)]
            pj_ps = [ctx.ps(f"at_pj{s}{i}", [128, 512]) for i in range(2)]
            for kind, wsrc in (("k", io["k_w"]), ("q", io["q_w"])):
                for oc in range(16):
                    slw = oc % 2
                    build_wT(ctx, wsrc, wsl[slw], ident, [wnatb[slw]], tp_ps, [oc],
                             dest_col0=0)
                    for ng in range(4):
                        psl = ng % 2
                        def mmqk(e, ng=ng, psl=psl, slw=slw):
                            for dc in range(16):
                                inst = e.matmul(out=pj_ps[psl].ap, lhsT=wsl[slw].ap[:, dc, :],
                                                rhs=xnf.ap[:, dc, ng * 512:(ng + 1) * 512],
                                                start=(dc == 0), stop=(dc == 15))
                            return inst
                        ctx.op("tensor", mmqk, reads=[wsl[slw], xnf], writes=[pj_ps[psl]])
                        if kind == "k":
                            ctx.op("vector", lambda e, oc=oc, ng=ng, psl=psl: e.tensor_copy(
                                K_res.ap[:, oc, ng * 512:(ng + 1) * 512], pj_ps[psl].ap),
                                reads=[pj_ps[psl]], writes=[K_res])
                        else:
                            qsl2 = ng % 2
                            ctx.op("vector", lambda e, psl=psl, qsl2=qsl2: e.tensor_copy(
                                qstage[qsl2].ap, pj_ps[psl].ap),
                                reads=[pj_ps[psl]], writes=[qstage[qsl2]])
                            ctx.op("sync", lambda e, oc=oc, ng=ng, qsl2=qsl2: e.dma_start(
                                out=qd[:, s, ng * 4:(ng + 1) * 4, oc, :],
                                in_=qstage[qsl2].ap.rearrange("p (qi q) -> p qi q", q=128)),
                                reads=[qstage[qsl2]], writes=[io["q_d"]], dma=True)

        if DBG_STAGE == "qk":
            return
        V_res = ctx.sb(f"at_V{s}", [128, 16, S], BF16)
        with ctx.scope():
            wv_full = ctx.sb(f"at_wv{s}", [128, 16, D], BF16)
            with ctx.scope():
                wnatb = [ctx.sb(f"at_wnb{s}{i}", [128, 2048], BF16) for i in range(2)]
                tp_ps = [ctx.ps(f"at_tp{s}{i}", [128, 1024], BF16) for i in range(2)]
                build_wT(ctx, io["v_w"], wv_full, ident, wnatb, tp_ps, list(range(16)))
            xn_tile = [ctx.sb(f"at_xt{s}{i}", [128, 16, 128], BF16) for i in range(2)]
            pj_ps = [ctx.ps(f"at_pj{s}{i}", [128, 512]) for i in range(2)]
            for tc in range(16):
                tsl = tc % 2
                ctx.op("sync", lambda e, tc=tc, tsl=tsl: e.dma_start(
                    out=xn_tile[tsl].ap, in_=xtd[:, s * 16 + tc, :, :]),
                    reads=[io["xn_t"]], writes=[xn_tile[tsl]], dma=True)
                for es in range(4):
                    psl = (tc * 4 + es) % 2
                    def mmv(e, es=es, psl=psl, tsl=tsl):
                        for dc in range(16):
                            inst = e.matmul(out=pj_ps[psl].ap, lhsT=xn_tile[tsl].ap[:, dc, :],
                                            rhs=wv_full.ap[:, dc, es * 512:(es + 1) * 512],
                                            start=(dc == 0), stop=(dc == 15))
                        return inst
                    ctx.op("tensor", mmv, reads=[xn_tile[tsl], wv_full], writes=[pj_ps[psl]])
                    ctx.op("vector", lambda e, tc=tc, es=es, psl=psl: e.tensor_copy(
                        V_res.ap[:, tc, es * 512:(es + 1) * 512], pj_ps[psl].ap),
                        reads=[pj_ps[psl]], writes=[V_res])

        if DBG_STAGE == "v":
            return
        nqi = 1 if DBG_STAGE == "attn1" else 16
        with ctx.scope():
            qt = [ctx.sb(f"at_qt{s}{i}", [128, 16, 128], BF16) for i in range(2)]
            P_sb = ctx.sb(f"at_P{s}", [128, S], BF16)
            PT_sb = ctx.sb(f"at_PT{s}", [128, S], BF16)
            mx = ctx.sb(f"at_mx{s}", [128, 1])
            negb = ctx.sb(f"at_nb{s}", [128, 1])
            lcol = ctx.sb(f"at_l{s}", [128, 1])
            rl = ctx.sb(f"at_rl{s}", [128, 1])
            tanh_sb = ctx.sb(f"at_th{s}", [128, 128])
            tp1_sb = ctx.sb(f"at_tp1{s}", [128, 128])
            ah1 = ctx.sb(f"at_h1{s}", [128, 1])
            ah2 = ctx.sb(f"at_h2{s}", [128, 1])
            gcol = [ctx.sb(f"at_gc{s}{i}", [128, 16, 128], BF16) for i in range(2)]
            sc_ps = ctx.ps(f"at_sc{s}", [128, 2048])
            pt_ps2 = [ctx.ps(f"at_pt{s}{i}", [128, 1024], BF16) for i in range(2)]
            av_ps = [ctx.ps(f"at_av{s}{i}", [128, 512]) for i in range(2)]
            gd = io["gated_d"].ap.rearrange("s qi p hh q -> p s qi hh q")
            qdl = io["q_d"].ap.rearrange("s qi p dc q -> p s qi dc q")

            for qi in range(nqi):
                qsl = qi % 2
                ctx.op("sync", lambda e, qi=qi, qsl=qsl: e.dma_start(
                    out=qt[qsl].ap, in_=qdl[:, s, qi, :, :]),
                    reads=[io["q_d"]], writes=[qt[qsl]], dma=True)
                for h in range(16):
                    it = qi * 16 + h
                    kcn = qi + 1
                    kn = kcn * 128
                    def mms(e, h=h, kn=kn, qsl=qsl):
                        for n0 in range(0, kn, 512):
                            n1 = min(n0 + 512, kn)
                            inst = e.matmul(out=sc_ps.ap[:, n0:n1], lhsT=qt[qsl].ap[:, h, :],
                                            rhs=K_res.ap[:, h, n0:n1], start=True, stop=True)
                        return inst
                    ctx.op("tensor", mms, reads=[qt[qsl], K_res], writes=[sc_ps])
                    ctx.op("vector", lambda e, qi=qi, kn=kn: e.tensor_tensor(
                        out=sc_ps.ap[:, qi * 128:kn], in0=sc_ps.ap[:, qi * 128:kn],
                        in1=mk_sb.ap, op=ALU.add), reads=[sc_ps, mk_sb], writes=[sc_ps])
                    ctx.op("vector", lambda e, kn=kn: e.reduce_max(
                        out=mx.ap, in_=sc_ps.ap[:, 0:kn], axis=AX.X), reads=[sc_ps], writes=[mx])
                    ctx.op("vector", lambda e: e.tensor_scalar(
                        out=negb.ap, in0=mx.ap, scalar1=-ATT_SCALE, scalar2=None, op0=ALU.mult),
                        reads=[mx], writes=[negb])
                    ctx.op("scalar", lambda e, kn=kn: e.activation(
                        out=P_sb.ap[:, 0:kn], in_=sc_ps.ap[:, 0:kn], func=AF.Exp,
                        bias=negb.ap, scale=ATT_SCALE, accum_out=lcol.ap),
                        reads=[sc_ps, negb], writes=[P_sb, lcol])
                    emit_recip(ctx, rl, lcol, ah1, ah2)
                    ctx.op("vector", lambda e, kn=kn: e.tensor_scalar(
                        out=P_sb.ap[:, 0:kn], in0=P_sb.ap[:, 0:kn], scalar1=rl.ap, scalar2=0.5,
                        op0=ALU.mult, op1=ALU.mult), reads=[P_sb, rl], writes=[P_sb])
                    for kc in range(kcn):
                        tsl = kc % 2
                        ctx.op("tensor", lambda e, kc=kc, tsl=tsl: e.transpose(
                            out=pt_ps2[tsl].ap[:, 0:128],
                            in_=P_sb.ap[:, kc * 128:(kc + 1) * 128],
                            identity=ident.ap), reads=[P_sb, ident], writes=[pt_ps2[tsl]])
                        ctx.op("scalar", lambda e, kc=kc, tsl=tsl: e.copy(
                            out=PT_sb.ap[:, kc * 128:(kc + 1) * 128],
                            in_=pt_ps2[tsl].ap[:, 0:128]),
                            reads=[pt_ps2[tsl]], writes=[PT_sb])
                    avsl = it % 2
                    gsl2 = (it + 1) % 2
                    def mmav(e, kcn=kcn, h=h, avsl=avsl):
                        for kc in range(kcn):
                            inst = e.matmul(out=av_ps[avsl].ap[:, 0:128],
                                            lhsT=V_res.ap[:, kc, h * 128:(h + 1) * 128],
                                            rhs=PT_sb.ap[:, kc * 128:(kc + 1) * 128],
                                            start=(kc == 0), stop=(kc == kcn - 1))
                        return inst
                    ctx.op("tensor", mmav, reads=[V_res, PT_sb], writes=[av_ps[avsl]])
                    ctx.op("tensor", lambda e, h=h, qsl=qsl, gsl2=gsl2: e.matmul(
                        out=av_ps[gsl2].ap[:, 0:128], lhsT=gw_sb.ap, rhs=qt[qsl].ap[:, h, :],
                        start=True, stop=True), reads=[gw_sb, qt[qsl]], writes=[av_ps[gsl2]])
                    ctx.op("scalar", lambda e, gsl2=gsl2: e.activation(
                        out=tanh_sb.ap, in_=av_ps[gsl2].ap[:, 0:128], func=AF.Tanh,
                        bias=gb_sb.ap, scale=0.5), reads=[av_ps[gsl2], gb_sb], writes=[tanh_sb])
                    ctx.op("vector", lambda e: e.tensor_scalar(
                        out=tp1_sb.ap, in0=tanh_sb.ap, scalar1=1.0, scalar2=None, op0=ALU.add),
                        reads=[tanh_sb], writes=[tp1_sb])
                    ctx.op("vector", lambda e, h=h, qsl=qsl, avsl=avsl: e.tensor_tensor(
                        out=gcol[qsl].ap[:, h, :], in0=tp1_sb.ap,
                        in1=av_ps[avsl].ap[:, 0:128], op=ALU.mult),
                        reads=[tp1_sb, av_ps[avsl]], writes=[gcol[qsl]])
                ctx.op("sync", lambda e, qi=qi, qsl=qsl: e.dma_start(
                    out=gd[:, s, qi, :, :], in_=gcol[qsl].ap),
                    reads=[gcol[qsl]], writes=[io["gated_d"]], dma=True)



def emit_ffn(ctx, io):
    ctx.phase = "f"
    with ctx.scope():
        _emit_ffn_body(ctx, io)


def _emit_ffn_body(ctx, io):
    nc = ctx.nc
    xn_tb = ctx.sb("f_xn", [128, 16, 1024], BF16)
    wfc_sb = ctx.sb("f_wfc", [128, 16, 1024], BF16)
    wpj_sb = ctx.sb("f_wpj", [128, 8, 2048], BF16)
    hsq = ctx.sb("f_hsq", [128, 8, 1024], BF16)
    r_sb = [ctx.sb(f"f_r{i}", [128, 512], BF16) for i in range(2)]
    acc = ctx.sb("f_acc", [128, 16, 1024])
    g_sb = [ctx.sb(f"f_g{i}", [128, 1024], BF16) for i in range(2)]
    h_ps = [ctx.ps(f"f_h{i}", [128, 512]) for i in range(2)]
    o_ps = [ctx.ps(f"f_o{i}", [128, 512]) for i in range(2)]

    xnd = io["xn_f"].ap.rearrange("(dc p) t -> p dc t", p=128)
    wfcd = io["cc_out_fc"].ap.rearrange("(dc p) o -> p dc o", p=128)
    wpjd = io["cc_out_pj"].ap.rearrange("(c p) e -> p c e", p=128)
    gdd = io["g_d"].ap.rearrange("(ec p) t -> p ec t", p=128)

    for tb in range(4):
        ctx.op("sync", lambda e, tb=tb: e.dma_start(
            out=xn_tb.ap, in_=xnd[:, :, tb * 1024:(tb + 1) * 1024]),
            reads=[io["xn_f"]], writes=[xn_tb], dma=True)
        for hc in range(8):
            ctx.op("sync", lambda e, hc=hc: e.dma_start(
                out=wfc_sb.ap, in_=wfcd[:, :, hc * 1024:(hc + 1) * 1024]),
                reads=[io["cc_out_fc"]], writes=[wfc_sb], dma=True)
            ctx.op("sync", lambda e, hc=hc: e.dma_start(
                out=wpj_sb.ap, in_=wpjd[:, hc * 8:(hc + 1) * 8, :]),
                reads=[io["cc_out_pj"]], writes=[wpj_sb], dma=True)
            for hoc in range(8):
                for ng in range(2):
                    psl = (hoc * 2 + ng) % 2
                    def mmh(e, hoc=hoc, ng=ng, psl=psl):
                        for dc in range(16):
                            inst = e.matmul(out=h_ps[psl].ap,
                                            lhsT=wfc_sb.ap[:, dc, hoc * 128:(hoc + 1) * 128],
                                            rhs=xn_tb.ap[:, dc, ng * 512:(ng + 1) * 512],
                                            start=(dc == 0), stop=(dc == 15))
                        return inst
                    ctx.op("tensor", mmh, reads=[wfc_sb, xn_tb], writes=[h_ps[psl]])
                    rsl = (hoc * 2 + ng) % 2
                    ctx.op("scalar", lambda e, psl=psl, rsl=rsl: e.activation(
                        out=r_sb[rsl].ap, in_=h_ps[psl].ap, func=AF.Relu),
                        reads=[h_ps[psl]], writes=[r_sb[rsl]])
                    ctx.op("vector", lambda e, hoc=hoc, ng=ng, rsl=rsl: e.tensor_tensor(
                        out=hsq.ap[:, hoc, ng * 512:(ng + 1) * 512], in0=r_sb[rsl].ap,
                        in1=r_sb[rsl].ap, op=ALU.mult), reads=[r_sb[rsl]], writes=[hsq])
            for ec in range(16):
                for ng in range(2):
                    osl = (ec * 2 + ng) % 2
                    def mmo(e, ec=ec, ng=ng, osl=osl):
                        for hsc in range(8):
                            inst = e.matmul(out=o_ps[osl].ap,
                                            lhsT=wpj_sb.ap[:, hsc, ec * 128:(ec + 1) * 128],
                                            rhs=hsq.ap[:, hsc, ng * 512:(ng + 1) * 512],
                                            start=(hsc == 0), stop=(hsc == 7))
                        return inst
                    ctx.op("tensor", mmo, reads=[wpj_sb, hsq], writes=[o_ps[osl]])
                    if hc == 0:
                        ctx.op("vector", lambda e, ec=ec, ng=ng, osl=osl: e.tensor_copy(
                            acc.ap[:, ec, ng * 512:(ng + 1) * 512], o_ps[osl].ap),
                            reads=[o_ps[osl]], writes=[acc])
                    else:
                        ctx.op("vector", lambda e, ec=ec, ng=ng, osl=osl: e.tensor_tensor(
                            out=acc.ap[:, ec, ng * 512:(ng + 1) * 512],
                            in0=acc.ap[:, ec, ng * 512:(ng + 1) * 512],
                            in1=o_ps[osl].ap, op=ALU.add),
                            reads=[acc, o_ps[osl]], writes=[acc])
        for ec in range(16):
            gsl = ec % 2
            ctx.op("scalar", lambda e, ec=ec, gsl=gsl: e.activation(
                out=g_sb[gsl].ap, in_=acc.ap[:, ec, :], func=AF.Silu),
                reads=[acc], writes=[g_sb[gsl]])
            ctx.op("sync", lambda e, ec=ec, tb=tb, gsl=gsl: e.dma_start(
                out=gdd[:, ec, tb * 1024:(tb + 1) * 1024], in_=g_sb[gsl].ap),
                reads=[g_sb[gsl]], writes=[io["g_d"]], dma=True)



def emit_comb(ctx, io, ident):
    ctx.phase = "c"
    with ctx.scope():
        _emit_comb_body(ctx, io, ident)


def _emit_comb_body(ctx, io, ident):
    nc = ctx.nc
    ow_full = ctx.sb("c_ow", [128, 16, D], BF16)
    with ctx.scope():
        wnatb = [ctx.sb(f"c_wnb{i}", [128, 2048], BF16) for i in range(2)]
        tp_ps = [ctx.ps(f"c_tp{i}", [128, 1024], BF16) for i in range(2)]
        build_wT(ctx, io["o_w"], ow_full, ident, wnatb, tp_ps, list(range(16)))
    g_tb = ctx.sb("c_g", [128, 16, 1024], BF16)
    gated_t = [ctx.sb(f"c_gt{i}", [128, 16, 128], BF16) for i in range(2)]
    sg = ctx.sb("c_sg", [128, 16, 128])
    xs = [ctx.sb(f"c_x{i}", [128, D]) for i in range(2)]
    out_sb = [ctx.sb(f"c_out{i}", [128, D]) for i in range(2)]
    s_ps = [ctx.ps(f"c_s{i}", [128, 512]) for i in range(2)]
    t_ps = [ctx.ps(f"c_t{i}", [128, 2048]) for i in range(1)]

    gdd = io["g_d"].ap.rearrange("(ec p) t -> p ec t", p=128)
    gtd = io["gated_d"].ap.rearrange("s qi p hh q -> p s qi hh q")
    x_t = io["x"].ap.rearrange("(t p) e -> p t e", p=128)
    y_t = io["y"].ap.rearrange("(t p) e -> p t e", p=128)

    if DBG_COMB == "ow":
        return
    for s in range(2):
        for tb in range(2):
            ctx.op("sync", lambda e, s=s, tb=tb: e.dma_start(
                out=g_tb.ap, in_=gdd[:, :, s * 2048 + tb * 1024:s * 2048 + (tb + 1) * 1024]),
                reads=[io["g_d"]], writes=[g_tb], dma=True)
            for qq in range(8):
                qi = tb * 8 + qq
                ti = s * 16 + qi
                sl = qq % 2
                ctx.op("sync", lambda e, s=s, qi=qi, sl=sl: e.dma_start(
                    out=gated_t[sl].ap, in_=gtd[:, s, qi, :, :]),
                    reads=[io["gated_d"]], writes=[gated_t[sl]], dma=True)
                ctx.op("sync", lambda e, ti=ti, sl=sl: e.dma_start(
                    out=xs[sl].ap, in_=x_t[:, ti, :]),
                    reads=[io["x"]], writes=[xs[sl]], dma=True)
                for ec in range(16):
                    osl = ec % 2
                    def mmo(e, ec=ec, sl=sl, osl=osl):
                        for dc in range(16):
                            inst = e.matmul(out=s_ps[osl].ap[:, 0:128],
                                            lhsT=ow_full.ap[:, dc, ec * 128:(ec + 1) * 128],
                                            rhs=gated_t[sl].ap[:, dc, :],
                                            start=(dc == 0), stop=(dc == 15))
                        return inst
                    ctx.op("tensor", mmo, reads=[ow_full, gated_t[sl]], writes=[s_ps[osl]])
                    ctx.op("vector", lambda e, ec=ec, qi=qi, osl=osl: e.tensor_tensor(
                        out=sg.ap[:, ec, :], in0=s_ps[osl].ap[:, 0:128],
                        in1=g_tb.ap[:, ec, (qi % 8) * 128:((qi % 8) + 1) * 128], op=ALU.mult),
                        reads=[s_ps[osl], g_tb], writes=[sg])
                if DBG_COMB == "mm":
                    ctx.op("vector", lambda e, sl=sl: e.tensor_copy(
                        out_sb[sl].ap[:, 0:2048],
                        sg.ap.rearrange("p a b -> p (a b)")), reads=[sg], writes=[out_sb[sl]])
                else:
                    def mtp(e):
                        for ec in range(16):
                            inst = e.transpose(out=t_ps[0].ap[:, ec * 128:(ec + 1) * 128],
                                               in_=sg.ap[:, ec, :], identity=io["identf"].ap)
                        return inst
                    ctx.op("tensor", mtp, reads=[sg, io["identf"]], writes=[t_ps[0]])
                    ctx.op("vector", lambda e, sl=sl: e.tensor_tensor(
                        out=out_sb[sl].ap, in0=xs[sl].ap, in1=t_ps[0].ap, op=ALU.add),
                        reads=[xs[sl], t_ps[0]], writes=[out_sb[sl]])
                ctx.op("sync", lambda e, ti=ti, sl=sl: e.dma_start(
                    out=y_t[:, ti, :], in_=out_sb[sl].ap),
                    reads=[out_sb[sl]], writes=[io["y"]], dma=True)



def finish(ctx):
    nc = ctx.nc
    for name, cnt in ctx.counts.items():
        if cnt > 0:
            nc.sync.wait_ge(ctx.sems[name], cnt)


def build_attn_nc():
    nc = bass.Bass("TRN2", target_bir_lowering=False)
    io = {}
    io["x"] = Buf(nc.dram_tensor("x", [4096, 2048], F32, kind="ExternalInput").ap())
    for n in ("q_w", "k_w", "v_w"):
        io[n] = Buf(nc.dram_tensor(n, [2048, 2048], F32, kind="ExternalInput").ap())
    io["gate_w"] = Buf(nc.dram_tensor("gate_w", [128, 128], F32, kind="ExternalInput").ap())
    io["gate_b"] = Buf(nc.dram_tensor("gate_b", [128, 1], F32, kind="ExternalInput").ap())
    io["identb"] = Buf(nc.dram_tensor("identb", [128, 128], BF16, kind="ExternalInput").ap())
    io["maskc"] = Buf(nc.dram_tensor("maskc", [128, 128], F32, kind="ExternalInput").ap())
    io["xn_f"] = Buf(nc.dram_tensor("xn_f", [2048, 4096], BF16, kind="ExternalOutput").ap())
    io["xn_t"] = Buf(nc.dram_tensor("xn_t", [32, 128, 16, 128], BF16, kind="Internal").ap())
    io["q_d"] = Buf(nc.dram_tensor("q_d", [2, 16, 128, 16, 128], BF16, kind="Internal").ap())
    io["gated_d"] = Buf(nc.dram_tensor("gated_d", [2, 16, 128, 16, 128], BF16,
                                       kind="ExternalOutput").ap())
    with ExitStack() as stack:
        ctx = Ctx(nc, stack)
        ctx.phase = "c0"
        ident = load_const(ctx, io["identb"], "s_id", [128, 128], BF16)
        mk_sb = load_const(ctx, io["maskc"], "s_mk", [128, 128], F32)
        gbraw = load_const(ctx, io["gate_b"], "s_gbr", [128, 1], F32)
        gb_sb = ctx.sb("s_gb", [128, 1])
        ctx.op("vector", lambda e: e.tensor_scalar(
            out=gb_sb.ap, in0=gbraw.ap, scalar1=0.5, scalar2=None, op0=ALU.mult),
            reads=[gbraw], writes=[gb_sb])
        gw_sb = ctx.sb("s_gw", [128, 128], BF16)
        with ctx.scope():
            gwn = load_const(ctx, io["gate_w"], "s_gwn", [128, 128], F32)
            gwnb = ctx.sb("s_gwnb", [128, 128], BF16)
            tp0 = ctx.ps("s_tp0", [128, 1024], BF16)
            ctx.op("vector", lambda e: e.tensor_copy(gwnb.ap, gwn.ap), reads=[gwn], writes=[gwnb])
            ctx.op("tensor", lambda e: e.transpose(out=tp0.ap[:, 0:128], in_=gwnb.ap,
                                                   identity=ident.ap),
                   reads=[gwnb, ident], writes=[tp0])
            ctx.op("scalar", lambda e: e.copy(out=gw_sb.ap, in_=tp0.ap[:, 0:128]),
                   reads=[tp0], writes=[gw_sb])

        emit_ln(ctx, io, ident)
        if DBG_STAGE != "ln":
            nseq = 2 if DBG_STAGE == "full" else 1
            for s in range(nseq):
                emit_attn_seq(ctx, io, s, ident, mk_sb, gw_sb, gb_sb)
        finish(ctx)
    return nc


def build_quant_nc():
    nc = bass.Bass("TRN2", target_bir_lowering=False)
    io = {}
    io["wfc_sl"] = Buf(nc.dram_tensor("wfc_sl", [8192, 256], F32, kind="ExternalInput").ap())
    io["wpj_sl"] = Buf(nc.dram_tensor("wpj_sl", [2048, 1024], F32, kind="ExternalInput").ap())
    io["identb"] = Buf(nc.dram_tensor("identb", [128, 128], BF16, kind="ExternalInput").ap())
    io["cc_in_fc"] = Buf(nc.dram_tensor("cc_in_fc", [256, 8192], BF16, kind="Internal").ap())
    io["cc_in_pj"] = Buf(nc.dram_tensor("cc_in_pj", [1024, 2048], BF16, kind="Internal").ap())
    io["cc_out_fc"] = Buf(nc.dram_tensor("cc_out_fc", [2048, 8192], BF16, kind="Internal",
                                         addr_space="Shared").ap())
    io["cc_out_pj"] = Buf(nc.dram_tensor("cc_out_pj", [8192, 2048], BF16, kind="Internal",
                                         addr_space="Shared").ap())
    wfcT = Buf(nc.dram_tensor("wfcT", [2048, 8192], BF16, kind="ExternalOutput").ap())
    wpjT = Buf(nc.dram_tensor("wpjT", [8192, 2048], BF16, kind="ExternalOutput").ap())
    with ExitStack() as stack:
        ctx = Ctx(nc, stack)
        ctx.phase = "c0"
        ident = load_const(ctx, io["identb"], "s_id", [128, 128], BF16)
        emit_quant(ctx, io, ident)
        ctx.phase = "qc"
        cp = ctx.sb("qc_cp", [128, 8192], BF16)
        for srcb, dstb, rows, cols in ((io["cc_out_fc"], wfcT, 2048, 8192),
                                       (io["cc_out_pj"], wpjT, 8192, 2048)):
            sv = srcb.ap.rearrange("(t p) o -> p t o", p=128)
            dv = dstb.ap.rearrange("(t p) o -> p t o", p=128)
            for t in range(rows // 128):
                ctx.op("sync", lambda e, t=t, sv=sv, cols=cols: e.dma_start(
                    out=cp.ap[:, 0:cols], in_=sv[:, t, :]),
                    reads=[srcb], writes=[cp], dma=True)
                ctx.op("sync", lambda e, t=t, dv=dv, cols=cols: e.dma_start(
                    out=dv[:, t, :], in_=cp.ap[:, 0:cols]),
                    reads=[cp], writes=[dstb], dma=True)
        finish(ctx)
    return nc


def build_ffn_nc():
    nc = bass.Bass("TRN2", target_bir_lowering=False)
    io = {}
    io["xn_f"] = Buf(nc.dram_tensor("xn_f", [2048, 4096], BF16, kind="ExternalInput").ap())
    io["cc_out_fc"] = Buf(nc.dram_tensor("wfcT", [2048, 8192], BF16, kind="ExternalInput").ap())
    io["cc_out_pj"] = Buf(nc.dram_tensor("wpjT", [8192, 2048], BF16, kind="ExternalInput").ap())
    io["g_d"] = Buf(nc.dram_tensor("g_d", [2048, 4096], BF16, kind="ExternalOutput").ap())
    with ExitStack() as stack:
        ctx = Ctx(nc, stack)
        emit_ffn(ctx, io)
        finish(ctx)
    return nc


def build_comb_nc():
    nc = bass.Bass("TRN2", target_bir_lowering=False)
    io = {}
    io["x"] = Buf(nc.dram_tensor("x", [4096, 2048], F32, kind="ExternalInput").ap())
    io["o_w"] = Buf(nc.dram_tensor("o_w", [2048, 2048], F32, kind="ExternalInput").ap())
    io["gated_d"] = Buf(nc.dram_tensor("gated_d", [2, 16, 128, 16, 128], BF16,
                                       kind="ExternalInput").ap())
    io["g_d"] = Buf(nc.dram_tensor("g_d", [2048, 4096], BF16, kind="ExternalInput").ap())
    io["identb"] = Buf(nc.dram_tensor("identb", [128, 128], BF16, kind="ExternalInput").ap())
    io["identf"] = Buf(nc.dram_tensor("identf", [128, 128], F32, kind="ExternalInput").ap())
    io["y"] = Buf(nc.dram_tensor("y", [4096, 2048], F32, kind="ExternalOutput").ap())
    with ExitStack() as stack:
        ctx = Ctx(nc, stack)
        ctx.phase = "c0"
        ident = load_const(ctx, io["identb"], "s_id", [128, 128], BF16)
        identf = load_const(ctx, io["identf"], "s_idf", [128, 128], F32)
        io["identf"] = identf
        emit_comb(ctx, io, ident)
        finish(ctx)
    return nc


def build_fused_nc():
    nc = bass.Bass("TRN2", target_bir_lowering=False)
    io = {}
    io["x"] = Buf(nc.dram_tensor("x", [4096, 2048], F32, kind="ExternalInput").ap())
    for n in ("q_w", "k_w", "v_w", "o_w"):
        io[n] = Buf(nc.dram_tensor(n, [2048, 2048], F32, kind="ExternalInput").ap())
    io["gate_w"] = Buf(nc.dram_tensor("gate_w", [128, 128], F32, kind="ExternalInput").ap())
    io["gate_b"] = Buf(nc.dram_tensor("gate_b", [128, 1], F32, kind="ExternalInput").ap())
    io["wfc_sl"] = Buf(nc.dram_tensor("wfc_sl", [8192, 256], F32, kind="ExternalInput").ap())
    io["wpj_sl"] = Buf(nc.dram_tensor("wpj_sl", [2048, 1024], F32, kind="ExternalInput").ap())
    io["identb"] = Buf(nc.dram_tensor("identb", [128, 128], BF16, kind="ExternalInput").ap())
    io["identf"] = Buf(nc.dram_tensor("identf", [128, 128], F32, kind="ExternalInput").ap())
    io["maskc"] = Buf(nc.dram_tensor("maskc", [128, 128], F32, kind="ExternalInput").ap())
    io["xn_f"] = Buf(nc.dram_tensor("xn_f", [2048, 4096], BF16, kind="Internal").ap())
    io["xn_t"] = Buf(nc.dram_tensor("xn_t", [32, 128, 16, 128], BF16, kind="Internal").ap())
    io["q_d"] = Buf(nc.dram_tensor("q_d", [2, 16, 128, 16, 128], BF16, kind="Internal").ap())
    io["gated_d"] = Buf(nc.dram_tensor("gated_d", [2, 16, 128, 16, 128], BF16,
                                       kind="Internal").ap())
    io["g_d"] = Buf(nc.dram_tensor("g_d", [2048, 4096], BF16, kind="Internal").ap())
    io["cc_in_fc"] = Buf(nc.dram_tensor("cc_in_fc", [256, 8192], BF16, kind="Internal").ap())
    io["cc_in_pj"] = Buf(nc.dram_tensor("cc_in_pj", [1024, 2048], BF16, kind="Internal").ap())
    io["cc_out_fc"] = Buf(nc.dram_tensor("cc_out_fc", [2048, 8192], BF16, kind="Internal",
                                         addr_space="Shared").ap())
    io["cc_out_pj"] = Buf(nc.dram_tensor("cc_out_pj", [8192, 2048], BF16, kind="Internal",
                                         addr_space="Shared").ap())
    io["y"] = Buf(nc.dram_tensor("y", [4096, 2048], F32, kind="ExternalOutput").ap())
    with ExitStack() as stack:
        ctx = Ctx(nc, stack)
        ctx.phase = "c0"
        ident = load_const(ctx, io["identb"], "s_id", [128, 128], BF16)
        mk_sb = load_const(ctx, io["maskc"], "s_mk", [128, 128], F32)
        identf = load_const(ctx, io["identf"], "s_idf", [128, 128], F32)
        io["identf"] = identf
        gbraw = load_const(ctx, io["gate_b"], "s_gbr", [128, 1], F32)
        gb_sb = ctx.sb("s_gb", [128, 1])
        ctx.op("vector", lambda e: e.tensor_scalar(
            out=gb_sb.ap, in0=gbraw.ap, scalar1=0.5, scalar2=None, op0=ALU.mult),
            reads=[gbraw], writes=[gb_sb])
        gw_sb = ctx.sb("s_gw", [128, 128], BF16)
        with ctx.scope():
            gwn = load_const(ctx, io["gate_w"], "s_gwn", [128, 128], F32)
            gwnb = ctx.sb("s_gwnb", [128, 128], BF16)
            tp0 = ctx.ps("s_tp0", [128, 1024], BF16)
            ctx.op("vector", lambda e: e.tensor_copy(gwnb.ap, gwn.ap), reads=[gwn], writes=[gwnb])
            ctx.op("tensor", lambda e: e.transpose(out=tp0.ap[:, 0:128], in_=gwnb.ap,
                                                   identity=ident.ap),
                   reads=[gwnb, ident], writes=[tp0])
            ctx.op("scalar", lambda e: e.copy(out=gw_sb.ap, in_=tp0.ap[:, 0:128]),
                   reads=[tp0], writes=[gw_sb])

        emit_quant(ctx, io, ident)
        emit_ln(ctx, io, ident)
        for s in range(2):
            emit_attn_seq(ctx, io, s, ident, mk_sb, gw_sb, gb_sb)
        emit_ffn(ctx, io)
        emit_comb(ctx, io, ident)
        finish(ctx)
    return nc



def kernel(L0_attn_x, L0_attn_q_w, L0_attn_k_w, L0_attn_v_w, L0_attn_o_w,
           L0_attn_gate_w, L0_attn_gate_b, L0_ffn_w_fc, L0_ffn_w_proj):
    x = np.ascontiguousarray(np.asarray(L0_attn_x, dtype=np.float32))
    q_w = np.ascontiguousarray(np.asarray(L0_attn_q_w, dtype=np.float32))
    k_w = np.ascontiguousarray(np.asarray(L0_attn_k_w, dtype=np.float32))
    v_w = np.ascontiguousarray(np.asarray(L0_attn_v_w, dtype=np.float32))
    o_w = np.ascontiguousarray(np.asarray(L0_attn_o_w, dtype=np.float32))
    gate_w = np.ascontiguousarray(np.asarray(L0_attn_gate_w, dtype=np.float32))
    gate_b = np.ascontiguousarray(np.asarray(L0_attn_gate_b, dtype=np.float32)).reshape(128, 1)
    w_fc = np.ascontiguousarray(np.asarray(L0_ffn_w_fc, dtype=np.float32))
    w_proj = np.ascontiguousarray(np.asarray(L0_ffn_w_proj, dtype=np.float32))

    identb = np.eye(128, dtype=np.float32).astype(ml_dtypes.bfloat16)
    identf = np.eye(128, dtype=np.float32)
    maskc = np.where(np.tril(np.ones((128, 128), dtype=bool)), 0.0, -1e30).astype(np.float32)

    cores = list(range(NCORE))
    xs = [np.ascontiguousarray(x[2 * c:2 * c + 2].reshape(4096, 2048)) for c in cores]
    wfc_sl = [np.ascontiguousarray(w_fc[:, 256 * c:256 * (c + 1)]) for c in cores]
    wpj_sl = [np.ascontiguousarray(w_proj[:, 1024 * c:1024 * (c + 1)]) for c in cores]

    nc = build_fused_nc()
    res = run_bass_kernel_spmd(
        nc, [{"x": xs[c], "q_w": q_w, "k_w": k_w, "v_w": v_w, "o_w": o_w,
              "gate_w": gate_w, "gate_b": gate_b,
              "wfc_sl": wfc_sl[c], "wpj_sl": wpj_sl[c],
              "identb": identb, "identf": identf, "maskc": maskc} for c in cores],
        core_ids=cores)
    ys = [np.asarray(res.results[c]["y"]) for c in cores]
    out = np.concatenate([y.reshape(2, 2048, 2048) for y in ys], axis=0)
    return out.astype(np.float32)


# version 03
# speedup vs baseline: 1.2030x; 1.2030x over previous
import numpy as np
import ml_dtypes
import concourse.bass as bass
import concourse.mybir as mybir
from concourse.bass_utils import run_bass_kernel_spmd
from contextlib import ExitStack

F32 = mybir.dt.float32
BF16 = mybir.dt.bfloat16
U8 = mybir.dt.uint8
AF = mybir.ActivationFunctionType
ALU = mybir.AluOpType
AX = mybir.AxisListType

MAGIC = 12582912.0
QMAX = 15.0
SHRINKS = [float(np.float32(s)) for s in np.linspace(0.4, 1.0, 16)]
ATT_SCALE = float(1.0 / np.sqrt(np.float32(128.0)))
EPS = 1e-5

D = 2048
S = 2048
NCORE = 8

DBG_STAGE = "full"
DBG_LN_TILES = 32
DBG_LN_PARTS = "all"
DBG_COMB = "full"



class Tok:
    __slots__ = ("sem", "val", "eng", "dma")
    def __init__(self, sem, val, eng, dma):
        self.sem, self.val, self.eng, self.dma = sem, val, eng, dma


class Buf:
    __slots__ = ("ap", "w", "r")
    def __init__(self, ap):
        self.ap = ap
        self.w = None
        self.r = []


class Ctx:
    def __init__(self, nc, stack):
        self.nc = nc
        self.stack = stack
        self.sems = {}
        self.counts = {}
        self.waited = {}
        self.phase = "p"

    def _sem(self, name):
        if name not in self.sems:
            self.sems[name] = self.stack.enter_context(self.nc.semaphore(name))
            self.counts[name] = 0
        return self.sems[name]

    def await_tok(self, eng, eng_name, tok):
        if tok is None:
            return
        if isinstance(tok, (list, tuple)):
            for t in tok:
                self.await_tok(eng, eng_name, t)
            return
        if (not tok.dma) and tok.eng == eng_name:
            return
        key = (eng_name, tok.sem)
        if self.waited.get(key, -1) >= tok.val:
            return
        eng.wait_ge(self.sems[tok.sem], tok.val)
        self.waited[key] = tok.val

    def op(self, eng_name, fn, reads=(), writes=(), extra=(), dma=False, sem=None):
        eng = getattr(self.nc, eng_name)
        toks = []
        for b in reads:
            toks.append(b.w)
        for b in writes:
            toks.append(b.w)
            toks.extend(b.r)
        toks.extend(extra)
        flat = []
        for t in toks:
            if t is None:
                continue
            if isinstance(t, (list, tuple)):
                flat.extend(x for x in t if x is not None)
            else:
                flat.append(t)
        will_wait = any(
            ((t.dma or t.eng != eng_name)
             and self.waited.get((eng_name, t.sem), -1) < t.val)
            for t in flat)
        if not hasattr(self, "_last_writes"):
            self._last_writes = {}
        hist = self._last_writes.get(eng_name, [])
        if will_wait:
            hist = []
        elif eng_name in ("vector", "scalar", "gpsimd") and not dma:
            recent = set().union(*hist) if hist else set()
            if any(id(b) in recent for b in list(reads) + list(writes)):
                eng.drain()
                hist = []
        for t in flat:
            self.await_tok(eng, eng_name, t)
        inst = fn(eng)
        if eng_name in ("vector", "scalar", "gpsimd") and not dma:
            hist = hist + [frozenset(id(b) for b in writes)]
            self._last_writes[eng_name] = hist[-2:]
        sname = sem or (self.phase + "_" + eng_name)
        h = self._sem(sname)
        k = 16 if dma else 1
        inst.then_inc(h, k)
        self.counts[sname] += k
        tok = Tok(sname, self.counts[sname], eng_name, dma)
        for b in reads:
            b.r.append(tok)
        for b in writes:
            b.w = tok
            b.r = []
        return tok

    _uid = [0]

    def _uname(self, name):
        self._uid[0] += 1
        return f"{name}_{self._uid[0]}"

    def sb(self, name, shape, dtype=F32):
        return Buf(self.stack.enter_context(
            self.nc.sbuf_tensor(self._uname(name), shape, dtype)).ap())

    def ps(self, name, shape, dtype=F32):
        return Buf(self.stack.enter_context(
            self.nc.psum_tensor(self._uname(name), shape, dtype)).ap())

    def scope(self):
        import contextlib

        @contextlib.contextmanager
        def _s():
            old = self.stack
            with ExitStack() as st:
                self.stack = st
                try:
                    yield
                finally:
                    self.barrier_all()
                    self.stack = old
        return _s()

    def barrier_all(self):
        for en in ("sync", "vector", "scalar", "tensor", "gpsimd"):
            eng = getattr(self.nc, en)
            for sname, cnt in self.counts.items():
                if cnt <= 0:
                    continue
                key = (en, sname)
                if self.waited.get(key, -1) >= cnt:
                    continue
                eng.wait_ge(self.sems[sname], cnt)
                self.waited[key] = cnt

    def dram(self, name, shape, dtype, kind="Internal", addr_space=None):
        if addr_space is None:
            t = self.nc.dram_tensor(name, shape, dtype, kind=kind)
        else:
            t = self.nc.dram_tensor(name, shape, dtype, kind=kind, addr_space=addr_space)
        return Buf(t.ap())


def emit_recip(ctx, out, in_, t1, t2):
    nc = ctx.nc
    ctx.op("scalar", lambda e: e.activation(out=t1.ap, in_=in_.ap, func=AF.Ln),
           reads=[in_], writes=[t1])
    ctx.op("scalar", lambda e: e.activation(out=t1.ap, in_=t1.ap, func=AF.Exp, scale=-1.0),
           reads=[t1], writes=[t1])
    ctx.op("vector", lambda e: e.tensor_tensor(out=t2.ap, in0=in_.ap, in1=t1.ap, op=ALU.mult),
           reads=[in_, t1], writes=[t2])
    ctx.op("vector", lambda e: e.tensor_scalar(out=t2.ap, in0=t2.ap, scalar1=-1.0, scalar2=2.0,
                                               op0=ALU.mult, op1=ALU.add),
           reads=[t2], writes=[t2])
    ctx.op("vector", lambda e: e.tensor_tensor(out=out.ap, in0=t1.ap, in1=t2.ap, op=ALU.mult),
           reads=[t1, t2], writes=[out])


def emit_rsqrt(ctx, out, in_, t1, t2):
    nc = ctx.nc
    ctx.op("scalar", lambda e: e.activation(out=t1.ap, in_=in_.ap, func=AF.Ln),
           reads=[in_], writes=[t1])
    ctx.op("scalar", lambda e: e.activation(out=t1.ap, in_=t1.ap, func=AF.Exp, scale=-0.5),
           reads=[t1], writes=[t1])
    ctx.op("vector", lambda e: e.tensor_tensor(out=t2.ap, in0=t1.ap, in1=t1.ap, op=ALU.mult),
           reads=[t1], writes=[t2])
    ctx.op("vector", lambda e: e.tensor_tensor(out=t2.ap, in0=in_.ap, in1=t2.ap, op=ALU.mult),
           reads=[in_, t2], writes=[t2])
    ctx.op("vector", lambda e: e.tensor_scalar(out=t2.ap, in0=t2.ap, scalar1=-0.5, scalar2=1.5,
                                               op0=ALU.mult, op1=ALU.add),
           reads=[t2], writes=[t2])
    ctx.op("vector", lambda e: e.tensor_tensor(out=out.ap, in0=t1.ap, in1=t2.ap, op=ALU.mult),
           reads=[t1, t2], writes=[out])


def view_blk(ap2d, w=64):
    return ap2d.rearrange("p (b w) -> p b w", w=w)


def bcast_blk(ap2d, nblk, w=64):
    return ap2d.rearrange("p (b o) -> p b o", o=1).broadcast_to([128, nblk, w])



def load_const(ctx, buf_dram, name, shape, dtype):
    b = ctx.sb(name, shape, dtype)
    ctx.op("sync", lambda e: e.dma_start(out=b.ap, in_=buf_dram.ap), reads=[buf_dram],
           writes=[b], dma=True)
    return b


def build_wT(ctx, w_nat, dest, ident, wnatb, tp_ps, oc_list, dest_col0=None,
             evac_eng="scalar"):
    wnv = w_nat.ap.rearrange("(t p) d -> p t d", p=128)
    for i, oc in enumerate(oc_list):
        sl = i % 2
        col0 = oc * 128 if dest_col0 is None else dest_col0 + i * 128
        ctx.op("gpsimd", lambda e, oc=oc, sl=sl: e.dma_start(
            out=wnatb[sl].ap, in_=wnv[:, oc, :]),
            reads=[w_nat], writes=[wnatb[sl]], dma=True)
        for dc in range(16):
            tsl = dc % 2
            ctx.op("tensor",
                   lambda e, sl=sl, dc=dc, tsl=tsl: e.transpose(
                       out=tp_ps[tsl].ap[:, 0:128],
                       in_=wnatb[sl].ap[:, dc * 128:(dc + 1) * 128], identity=ident.ap),
                   reads=[wnatb[sl], ident], writes=[tp_ps[tsl]])
            ctx.op(evac_eng,
                   lambda e, col0=col0, dc=dc, tsl=tsl: (
                       e.copy(out=dest.ap[:, dc, col0:col0 + 128],
                              in_=tp_ps[tsl].ap[:, 0:128])
                       if evac_eng == "scalar" else
                       e.tensor_copy(dest.ap[:, dc, col0:col0 + 128],
                                     tp_ps[tsl].ap[:, 0:128])),
                   reads=[tp_ps[tsl]], writes=[dest])



def emit_quant(ctx, io, ident):
    ctx.phase = "q"
    with ctx.scope():
        _emit_quant_body(ctx, io, ident)


def _emit_quant_body(ctx, io, ident):
    nc = ctx.nc
    wbuf = ctx.sb("q_w", [128, 8192])
    tbuf = [ctx.sb(f"q_t{i}", [128, 8192]) for i in range(2)]
    qbuf = [ctx.sb(f"q_q{i}", [128, 8192]) for i in range(2)]
    stage = ctx.sb("q_stage", [128, 16384], BF16)
    amax = ctx.sb("q_amax", [128, 128])
    base = ctx.sb("q_base", [128, 128])
    sc = [ctx.sb(f"q_sc{i}", [128, 128]) for i in range(2)]
    sc2 = [ctx.sb(f"q_sc2{i}", [128, 128]) for i in range(2)]
    rs = [ctx.sb(f"q_rs{i}", [128, 128]) for i in range(2)]
    errsl = ctx.sb("q_err", [128, 128])
    b_err = ctx.sb("q_berr", [128, 128])
    b_rs = ctx.sb("q_brs", [128, 128])
    b_sc = ctx.sb("q_bsc", [128, 128])
    mask8 = ctx.sb("q_mask", [128, 128], U8)
    qh1 = ctx.sb("q_h1", [128, 128])
    qh2 = ctx.sb("q_h2", [128, 128])
    dqs = [ctx.sb(f"q_dqs{i}", [128, 128], BF16) for i in range(2)]
    tp_ps = [ctx.ps(f"q_tp{i}", [128, 1024], BF16) for i in range(2)]

    gath_toks = []
    slice_idx = [0]
    for wname, wsrc, cc_in, ntile_h, icols in (
        ("fc", io["wfc_sl"], io["cc_in_fc"], 32, 256),
        ("pj", io["wpj_sl"], io["cc_in_pj"], 8, 1024),
    ):
        nhalf = 2
        o_total = ntile_h * nhalf * 128
        njj = icols // 128
        ncol = ntile_h * icols
        nblk = ncol // 64
        stage_v = stage.ap.rearrange("p (jj o) -> p jj o", o=o_total)
        for g in range(nhalf):
            src = wsrc.ap.rearrange("(t p) i -> p t i", p=128)[:, g * ntile_h:(g + 1) * ntile_h, :]
            ctx.op("sync", lambda e: e.dma_start(
                out=wbuf.ap.rearrange("p (t i) -> p t i", i=icols)[:, 0:ntile_h, :], in_=src),
                reads=[wsrc], writes=[wbuf], dma=True)
            half = wbuf.ap[:, 0:ncol]
            ctx.op("vector", lambda e: e.tensor_reduce(
                out=amax.ap, in_=view_blk(half), axis=AX.X, op=ALU.max,
                apply_absolute_value=True), reads=[wbuf], writes=[amax])
            ctx.op("vector", lambda e: e.tensor_scalar(
                out=base.ap, in0=amax.ap, scalar1=1e-8, scalar2=1.0 / 15.0,
                op0=ALU.max, op1=ALU.mult), reads=[amax], writes=[base])
            t_eng = "gpsimd" if slice_idx[0] % 2 == 0 else "vector"
            d_eng = "vector" if slice_idx[0] % 2 == 0 else "gpsimd"
            slice_idx[0] += 1
            for i, s_i in enumerate(SHRINKS):
                pp = i % 2
                tb, qb = tbuf[pp], qbuf[pp]
                ctx.op("vector", lambda e, s_i=s_i, pp=pp: e.tensor_scalar(
                    out=sc[pp].ap, in0=base.ap, scalar1=s_i, scalar2=None, op0=ALU.mult),
                    reads=[base], writes=[sc[pp]])
                emit_recip(ctx, rs[pp], sc[pp], qh1, qh2)
                ctx.op("vector", lambda e, pp=pp: e.tensor_tensor(
                    out=sc2[pp].ap, in0=sc[pp].ap, in1=sc[pp].ap, op=ALU.mult),
                    reads=[sc[pp]], writes=[sc2[pp]])
                ctx.op(t_eng, lambda e, pp=pp, tb=tb: e.tensor_tensor(
                    out=view_blk(tb.ap[:, 0:ncol]), in0=view_blk(half),
                    in1=bcast_blk(rs[pp].ap, nblk), op=ALU.mult),
                    reads=[wbuf, rs[pp]], writes=[tb])
                ctx.op("vector", lambda e, tb=tb, qb=qb: e.tensor_scalar(
                    out=qb.ap[:, 0:ncol], in0=tb.ap[:, 0:ncol], scalar1=MAGIC,
                    scalar2=MAGIC + QMAX, op0=ALU.add, op1=ALU.min),
                    reads=[tb], writes=[qb])
                ctx.op("vector", lambda e, qb=qb: e.tensor_scalar(
                    out=qb.ap[:, 0:ncol], in0=qb.ap[:, 0:ncol], scalar1=MAGIC - QMAX,
                    scalar2=MAGIC, op0=ALU.max, op1=ALU.subtract),
                    reads=[qb], writes=[qb])
                ctx.op(d_eng, lambda e, tb=tb, qb=qb: (
                    e.tensor_sub(qb.ap[:, 0:ncol], qb.ap[:, 0:ncol], tb.ap[:, 0:ncol])),
                    reads=[qb, tb], writes=[qb])
                ctx.op("scalar", lambda e, qb=qb: e.activation(
                    out=qb.ap[:, 0:ncol], in_=qb.ap[:, 0:ncol], func=AF.Square),
                    reads=[qb], writes=[qb])
                ctx.op("vector", lambda e, qb=qb: e.tensor_reduce(
                    out=errsl.ap, in_=view_blk(qb.ap[:, 0:ncol]), axis=AX.X, op=ALU.add),
                    reads=[qb], writes=[errsl])
                ctx.op("vector", lambda e, pp=pp: e.tensor_tensor(
                    out=errsl.ap, in0=errsl.ap, in1=sc2[pp].ap, op=ALU.mult),
                    reads=[errsl, sc2[pp]], writes=[errsl])
                if i == 0:
                    ctx.op("vector", lambda e: e.tensor_copy(b_err.ap, errsl.ap),
                           reads=[errsl], writes=[b_err])
                    ctx.op("vector", lambda e, pp=pp: e.tensor_copy(b_rs.ap, rs[pp].ap),
                           reads=[rs[pp]], writes=[b_rs])
                    ctx.op("vector", lambda e, pp=pp: e.tensor_copy(b_sc.ap, sc[pp].ap),
                           reads=[sc[pp]], writes=[b_sc])
                else:
                    ctx.op("vector", lambda e: e.tensor_tensor(
                        out=mask8.ap, in0=errsl.ap, in1=b_err.ap, op=ALU.is_lt),
                        reads=[errsl, b_err], writes=[mask8])
                    ctx.op("vector", lambda e: e.select(
                        out=b_err.ap, mask=mask8.ap, on_true=errsl.ap, on_false=b_err.ap),
                        reads=[mask8, errsl, b_err], writes=[b_err])
                    ctx.op("vector", lambda e, pp=pp: e.select(
                        out=b_rs.ap, mask=mask8.ap, on_true=rs[pp].ap, on_false=b_rs.ap),
                        reads=[mask8, rs[pp], b_rs], writes=[b_rs])
                    ctx.op("vector", lambda e, pp=pp: e.select(
                        out=b_sc.ap, mask=mask8.ap, on_true=sc[pp].ap, on_false=b_sc.ap),
                        reads=[mask8, sc[pp], b_sc], writes=[b_sc])
            ctx.op("vector", lambda e: e.tensor_tensor(
                out=view_blk(tbuf[0].ap[:, 0:ncol]), in0=view_blk(half),
                in1=bcast_blk(b_rs.ap, nblk), op=ALU.mult), reads=[wbuf, b_rs], writes=[tbuf[0]])
            ctx.op("vector", lambda e: e.tensor_scalar(
                out=qbuf[0].ap[:, 0:ncol], in0=tbuf[0].ap[:, 0:ncol], scalar1=MAGIC,
                scalar2=MAGIC + QMAX, op0=ALU.add, op1=ALU.min), reads=[tbuf[0]], writes=[qbuf[0]])
            ctx.op("vector", lambda e: e.tensor_scalar(
                out=qbuf[0].ap[:, 0:ncol], in0=qbuf[0].ap[:, 0:ncol], scalar1=MAGIC - QMAX,
                scalar2=MAGIC, op0=ALU.max, op1=ALU.subtract), reads=[qbuf[0]], writes=[qbuf[0]])
            for t in range(ntile_h):
                for jj in range(njj):
                    c0 = t * icols + jj * 128
                    slot = (t * njj + jj) % 2
                    ctx.op("vector", lambda e, c0=c0, slot=slot: e.tensor_tensor(
                        out=view_blk(dqs[slot].ap), in0=view_blk(qbuf[0].ap[:, c0:c0 + 128]),
                        in1=bcast_blk(b_sc.ap[:, c0 // 64:c0 // 64 + 2], 2), op=ALU.mult),
                        reads=[qbuf[0], b_sc], writes=[dqs[slot]])
                    ctx.op("tensor", lambda e, slot=slot: e.transpose(
                        out=tp_ps[slot].ap[:, 0:128], in_=dqs[slot].ap, identity=ident.ap),
                        reads=[dqs[slot], ident], writes=[tp_ps[slot]])
                    dst = stage_v[:, jj, g * ntile_h * 128 + t * 128:
                                  g * ntile_h * 128 + (t + 1) * 128]
                    ctx.op("scalar", lambda e, slot=slot, dst=dst: e.copy(
                        out=dst, in_=tp_ps[slot].ap[:, 0:128]),
                        reads=[tp_ps[slot]], writes=[stage])
        for jj in range(njj):
            ctx.op("sync", lambda e, jj=jj: e.dma_start(
                out=cc_in.ap.rearrange("(jj p) o -> p jj o", p=128)[:, jj, :],
                in_=stage_v[:, jj, 0:o_total]),
                reads=[stage], writes=[cc_in], dma=True)
    rg = [list(range(NCORE))]
    for cc_in, cc_out in ((io["cc_in_fc"], io["cc_out_fc"]), (io["cc_in_pj"], io["cc_out_pj"])):
        ctx.op("gpsimd", lambda e, a=cc_in, b=cc_out: e.collective_compute(
            "AllGather", ALU.bypass, replica_groups=rg, ins=[a.ap], outs=[b.ap]),
            reads=[cc_in], writes=[cc_out])



def emit_ln(ctx, io, ident):
    ctx.phase = "ln"
    nc = ctx.nc
    with ctx.scope():
        _emit_ln_body(ctx, io, ident)


def _emit_ln_body(ctx, io, ident):
    nc = ctx.nc
    xt = [ctx.sb(f"ln_x{i}", [128, D]) for i in range(2)]
    s1 = ctx.sb("ln_s1", [128, 1])
    s2 = ctx.sb("ln_s2", [128, 1])
    mean = ctx.sb("ln_mean", [128, 1])
    msq = ctx.sb("ln_msq", [128, 1])
    c1 = ctx.sb("ln_c1", [128, 1])
    h1 = ctx.sb("ln_h1", [128, 1])
    h2 = ctx.sb("ln_h2", [128, 1])
    rstd = ctx.sb("ln_rstd", [128, 1])
    negmb = ctx.sb("ln_negmb", [128, 1])
    xn_tm = ctx.sb("ln_xntm", [128, D], BF16)
    xstage = [ctx.sb(f"ln_stg{i}", [128, 16, 512], BF16) for i in range(2)]
    tstage = [ctx.sb(f"ln_tstg{i}", [128, 16, 128], BF16) for i in range(2)]
    xp_ps = [ctx.ps(f"ln_ps{i}", [128, 1024], BF16) for i in range(2)]

    xnd = io["xn_f"].ap.rearrange("(dc p) t -> p dc t", p=128)
    xtd = io["xn_t"].ap.rearrange("tt p dc t -> p tt dc t")
    x_t = io["x"].ap.rearrange("(t p) e -> p t e", p=128)

    for ti in range(DBG_LN_TILES):
        sl = ti % 2
        grp = ti // 4
        gsl = grp % 2
        ctx.op("sync", lambda e, sl=sl, ti=ti: e.dma_start(out=xt[sl].ap, in_=x_t[:, ti, :]),
               reads=[io["x"]], writes=[xt[sl]], dma=True)
        ctx.op("vector", lambda e, sl=sl: e.reduce_sum(
            out=s1.ap, in_=xt[sl].ap, axis=AX.X), reads=[xt[sl]], writes=[s1])
        ctx.op("scalar", lambda e, sl=sl: e.activation(
            out=xn_tm.ap, in_=xt[sl].ap, func=AF.Square, accum_out=s2.ap),
            reads=[xt[sl]], writes=[xn_tm, s2])
        ctx.op("vector", lambda e: e.tensor_scalar(
            out=mean.ap, in0=s1.ap, scalar1=1.0 / D, scalar2=None, op0=ALU.mult),
            reads=[s1], writes=[mean])
        ctx.op("vector", lambda e: e.tensor_tensor(
            out=msq.ap, in0=mean.ap, in1=mean.ap, op=ALU.mult), reads=[mean], writes=[msq])
        ctx.op("vector", lambda e: e.tensor_scalar(
            out=c1.ap, in0=s2.ap, scalar1=1.0 / D, scalar2=None, op0=ALU.mult),
            reads=[s2], writes=[c1])
        ctx.op("vector", lambda e: e.tensor_tensor(
            out=c1.ap, in0=c1.ap, in1=msq.ap, op=ALU.subtract), reads=[c1, msq], writes=[c1])
        ctx.op("vector", lambda e: e.tensor_scalar(
            out=c1.ap, in0=c1.ap, scalar1=EPS, scalar2=None, op0=ALU.add),
            reads=[c1], writes=[c1])
        emit_rsqrt(ctx, rstd, c1, h1, h2)
        ctx.op("vector", lambda e: e.tensor_tensor(
            out=negmb.ap, in0=mean.ap, in1=rstd.ap, op=ALU.mult), reads=[mean, rstd], writes=[negmb])
        ctx.op("vector", lambda e: e.tensor_scalar(
            out=negmb.ap, in0=negmb.ap, scalar1=-1.0, scalar2=None, op0=ALU.mult),
            reads=[negmb], writes=[negmb])
        ctx.op("scalar", lambda e, sl=sl: e.activation(
            out=xn_tm.ap, in_=xt[sl].ap, func=AF.Identity, bias=negmb.ap, scale=rstd.ap),
            reads=[xt[sl], negmb, rstd], writes=[xn_tm])
        if DBG_LN_PARTS == "math":
            continue
        for dc in range(16):
            tsl = dc % 2
            ctx.op("tensor", lambda e, dc=dc, tsl=tsl: e.transpose(
                out=xp_ps[tsl].ap[:, 0:128], in_=xn_tm.ap[:, dc * 128:(dc + 1) * 128],
                identity=ident.ap), reads=[xn_tm, ident], writes=[xp_ps[tsl]])
            tv = ctx.op("vector", lambda e, dc=dc, tsl=tsl, gsl=gsl, ti=ti: e.tensor_copy(
                xstage[gsl].ap[:, dc, (ti % 4) * 128:(ti % 4 + 1) * 128],
                xp_ps[tsl].ap[:, 0:128]), reads=[xp_ps[tsl]], writes=[xstage[gsl]])
            if DBG_LN_PARTS not in ("notstage",):
                ctx.op("scalar", lambda e, dc=dc, tsl=tsl, sl=sl: e.copy(
                    out=tstage[sl].ap[:, dc, :], in_=xp_ps[tsl].ap[:, 0:128]),
                    reads=[xp_ps[tsl]], writes=[tstage[sl]], extra=[tv])
        if DBG_LN_PARTS not in ("notstage", "nodma"):
            ctx.op("sync", lambda e, ti=ti, sl=sl: e.dma_start(
                out=xtd[:, ti, :, :], in_=tstage[sl].ap),
                reads=[tstage[sl]], writes=[io["xn_t"]], dma=True)
        if ti % 4 == 3:
            ctx.op("sync", lambda e, grp=grp, gsl=gsl: e.dma_start(
                out=xnd[:, :, grp * 512:(grp + 1) * 512], in_=xstage[gsl].ap),
                reads=[xstage[gsl]], writes=[io["xn_f"]], dma=True)


def emit_attn_seq(ctx, io, s, ident, mk_sb, gw_sb, gb_sb):
    nc = ctx.nc
    ctx.phase = f"a{s}"
    with ctx.scope():
        K_res = ctx.sb(f"at_K{s}", [128, 16, S], BF16)
        xnd = io["xn_f"].ap.rearrange("(dc p) t -> p dc t", p=128)
        xtd = io["xn_t"].ap.rearrange("tt p dc t -> p tt dc t")
        qd = io["q_d"].ap.rearrange("s qi p dc q -> p s qi dc q")

        with ctx.scope():
            xnf = ctx.sb(f"at_xnf{s}", [128, 16, S], BF16)
            ctx.op("sync", lambda e: e.dma_start(out=xnf.ap, in_=xnd[:, :, s * S:(s + 1) * S]),
                   reads=[io["xn_f"]], writes=[xnf], dma=True)
            wsl = [ctx.sb(f"at_w{s}{i}", [128, 16, 128], BF16) for i in range(2)]
            wnatb = [ctx.sb(f"at_wnb{s}{i}", [128, 2048], BF16) for i in range(2)]
            qstage = [ctx.sb(f"at_qs{s}{i}", [128, 512], BF16) for i in range(2)]
            gstage = [ctx.sb(f"at_gs{s}{i}", [128, 512], BF16) for i in range(2)]
            tp_ps = [ctx.ps(f"at_tp{s}{i}", [128, 1024], BF16) for i in range(2)]
            pj_ps = [ctx.ps(f"at_pj{s}{i}", [128, 512]) for i in range(2)]
            g_ps = [ctx.ps(f"at_gp{s}{i}", [128, 512]) for i in range(2)]
            gd2 = io["gate_d"].ap.rearrange("s qi p hh q -> p s qi hh q")
            for kind, wsrc in (("k", io["k_w"]), ("q", io["q_w"])):
                for oc in range(16):
                    slw = oc % 2
                    build_wT(ctx, wsrc, wsl[slw], ident, [wnatb[slw]], tp_ps, [oc],
                             dest_col0=0)
                    for ng in range(4):
                        psl = ng % 2
                        def mmqk(e, ng=ng, psl=psl, slw=slw):
                            for dc in range(16):
                                inst = e.matmul(out=pj_ps[psl].ap, lhsT=wsl[slw].ap[:, dc, :],
                                                rhs=xnf.ap[:, dc, ng * 512:(ng + 1) * 512],
                                                start=(dc == 0), stop=(dc == 15))
                            return inst
                        ctx.op("tensor", mmqk, reads=[wsl[slw], xnf], writes=[pj_ps[psl]])
                        if kind == "k":
                            ctx.op("vector", lambda e, oc=oc, ng=ng, psl=psl: e.tensor_copy(
                                K_res.ap[:, oc, ng * 512:(ng + 1) * 512], pj_ps[psl].ap),
                                reads=[pj_ps[psl]], writes=[K_res])
                        else:
                            qsl2 = ng % 2
                            ctx.op("vector", lambda e, psl=psl, qsl2=qsl2: e.tensor_copy(
                                qstage[qsl2].ap, pj_ps[psl].ap),
                                reads=[pj_ps[psl]], writes=[qstage[qsl2]])
                            ctx.op("sync", lambda e, oc=oc, ng=ng, qsl2=qsl2: e.dma_start(
                                out=qd[:, s, ng * 4:(ng + 1) * 4, oc, :],
                                in_=qstage[qsl2].ap.rearrange("p (qi q) -> p qi q", q=128)),
                                reads=[qstage[qsl2]], writes=[io["q_d"]], dma=True)
                            ctx.op("tensor", lambda e, psl=psl, qsl2=qsl2: e.matmul(
                                out=g_ps[psl].ap, lhsT=gw_sb.ap, rhs=qstage[qsl2].ap,
                                start=True, stop=True),
                                reads=[gw_sb, qstage[qsl2]], writes=[g_ps[psl]])
                            ctx.op("scalar", lambda e, psl=psl, qsl2=qsl2: e.activation(
                                out=gstage[qsl2].ap, in_=g_ps[psl].ap, func=AF.Tanh,
                                bias=gb_sb.ap, scale=0.5),
                                reads=[g_ps[psl], gb_sb], writes=[gstage[qsl2]])
                            ctx.op("vector", lambda e, qsl2=qsl2: e.tensor_scalar(
                                out=gstage[qsl2].ap, in0=gstage[qsl2].ap, scalar1=1.0,
                                scalar2=None, op0=ALU.add),
                                reads=[gstage[qsl2]], writes=[gstage[qsl2]])
                            ctx.op("sync", lambda e, oc=oc, ng=ng, qsl2=qsl2: e.dma_start(
                                out=gd2[:, s, ng * 4:(ng + 1) * 4, oc, :],
                                in_=gstage[qsl2].ap.rearrange("p (qi q) -> p qi q", q=128)),
                                reads=[gstage[qsl2]], writes=[io["gate_d"]], dma=True)

        if DBG_STAGE == "qk":
            return
        V_res = ctx.sb(f"at_V{s}", [128, 16, S], BF16)
        with ctx.scope():
            wv_full = ctx.sb(f"at_wv{s}", [128, 16, D], BF16)
            with ctx.scope():
                wnatb = [ctx.sb(f"at_wnb{s}{i}", [128, 2048], BF16) for i in range(2)]
                tp_ps = [ctx.ps(f"at_tp{s}{i}", [128, 1024], BF16) for i in range(2)]
                build_wT(ctx, io["v_w"], wv_full, ident, wnatb, tp_ps, list(range(16)))
            xn_tile = [ctx.sb(f"at_xt{s}{i}", [128, 16, 128], BF16) for i in range(2)]
            pj_ps = [ctx.ps(f"at_pj{s}{i}", [128, 512]) for i in range(2)]
            for tc in range(16):
                tsl = tc % 2
                ctx.op("sync", lambda e, tc=tc, tsl=tsl: e.dma_start(
                    out=xn_tile[tsl].ap, in_=xtd[:, s * 16 + tc, :, :]),
                    reads=[io["xn_t"]], writes=[xn_tile[tsl]], dma=True)
                for es in range(4):
                    psl = (tc * 4 + es) % 2
                    def mmv(e, es=es, psl=psl, tsl=tsl):
                        for dc in range(16):
                            inst = e.matmul(out=pj_ps[psl].ap, lhsT=xn_tile[tsl].ap[:, dc, :],
                                            rhs=wv_full.ap[:, dc, es * 512:(es + 1) * 512],
                                            start=(dc == 0), stop=(dc == 15))
                        return inst
                    ctx.op("tensor", mmv, reads=[xn_tile[tsl], wv_full], writes=[pj_ps[psl]])
                    ctx.op("vector", lambda e, tc=tc, es=es, psl=psl: e.tensor_copy(
                        V_res.ap[:, tc, es * 512:(es + 1) * 512], pj_ps[psl].ap),
                        reads=[pj_ps[psl]], writes=[V_res])

        if DBG_STAGE == "v":
            return
        nqi = 1 if DBG_STAGE == "attn1" else 16
        with ctx.scope():
            qt = [ctx.sb(f"at_qt{s}{i}", [128, 16, 128], BF16) for i in range(2)]
            gt = [ctx.sb(f"at_gt{s}{i}", [128, 16, 128], BF16) for i in range(2)]
            P_sb = [ctx.sb(f"at_P{s}{i}", [128, S], BF16) for i in range(2)]
            PT_sb = ctx.sb(f"at_PT{s}", [128, S], BF16)
            mx = [ctx.sb(f"at_mx{s}{i}", [128, 1]) for i in range(2)]
            negb = [ctx.sb(f"at_nb{s}{i}", [128, 1]) for i in range(2)]
            lcol = [ctx.sb(f"at_l{s}{i}", [128, 1]) for i in range(2)]
            rl = [ctx.sb(f"at_rl{s}{i}", [128, 1]) for i in range(2)]
            ah1 = ctx.sb(f"at_h1{s}", [128, 1])
            ah2 = ctx.sb(f"at_h2{s}", [128, 1])
            gcol = [ctx.sb(f"at_gc{s}{i}", [128, 16, 128], BF16) for i in range(2)]
            sc_ps = ctx.ps(f"at_sc{s}", [128, 2048])
            pt_ps2 = [ctx.ps(f"at_pt{s}{i}", [128, 1024], BF16) for i in range(2)]
            av_ps = [ctx.ps(f"at_av{s}{i}", [128, 512]) for i in range(2)]
            gd = io["gated_d"].ap.rearrange("s qi p hh q -> p s qi hh q")
            qdl = io["q_d"].ap.rearrange("s qi p dc q -> p s qi dc q")
            gdl = io["gate_d"].ap.rearrange("s qi p hh q -> p s qi hh q")

            def softmax_front(it):
                qi, h = divmod(it, 16)
                qsl = qi % 2
                pp = it % 2
                kcn = qi + 1
                kn = kcn * 128
                def mms(e, h=h, kn=kn, qsl=qsl):
                    for n0 in range(0, kn, 512):
                        n1 = min(n0 + 512, kn)
                        inst = e.matmul(out=sc_ps.ap[:, n0:n1], lhsT=qt[qsl].ap[:, h, :],
                                        rhs=K_res.ap[:, h, n0:n1], start=True, stop=True)
                    return inst
                ctx.op("tensor", mms, reads=[qt[qsl], K_res], writes=[sc_ps])
                ctx.op("vector", lambda e, qi=qi, kn=kn: e.tensor_tensor(
                    out=sc_ps.ap[:, qi * 128:kn], in0=sc_ps.ap[:, qi * 128:kn],
                    in1=mk_sb.ap, op=ALU.add), reads=[sc_ps, mk_sb], writes=[sc_ps])
                ctx.op("vector", lambda e, kn=kn, pp=pp: e.reduce_max(
                    out=mx[pp].ap, in_=sc_ps.ap[:, 0:kn], axis=AX.X),
                    reads=[sc_ps], writes=[mx[pp]])
                ctx.op("vector", lambda e, pp=pp: e.tensor_scalar(
                    out=negb[pp].ap, in0=mx[pp].ap, scalar1=-ATT_SCALE, scalar2=None,
                    op0=ALU.mult), reads=[mx[pp]], writes=[negb[pp]])
                ctx.op("scalar", lambda e, kn=kn, pp=pp: e.activation(
                    out=P_sb[pp].ap[:, 0:kn], in_=sc_ps.ap[:, 0:kn], func=AF.Exp,
                    bias=negb[pp].ap, scale=ATT_SCALE, accum_out=lcol[pp].ap),
                    reads=[sc_ps, negb[pp]], writes=[P_sb[pp], lcol[pp]])
                emit_recip(ctx, rl[pp], lcol[pp], ah1, ah2)
                ctx.op("vector", lambda e, kn=kn, pp=pp: e.tensor_scalar(
                    out=P_sb[pp].ap[:, 0:kn], in0=P_sb[pp].ap[:, 0:kn], scalar1=rl[pp].ap,
                    scalar2=0.5, op0=ALU.mult, op1=ALU.mult),
                    reads=[P_sb[pp], rl[pp]], writes=[P_sb[pp]])

            def attend_back(it):
                qi, h = divmod(it, 16)
                qsl = qi % 2
                pp = it % 2
                kcn = qi + 1
                for g0 in range(0, kcn, 4):
                    gn = min(4, kcn - g0)
                    slot = (g0 // 4) % 2
                    for j in range(gn):
                        kc = g0 + j
                        ctx.op("tensor", lambda e, kc=kc, j=j, slot=slot, pp=pp: e.transpose(
                            out=pt_ps2[slot].ap[:, j * 128:(j + 1) * 128],
                            in_=P_sb[pp].ap[:, kc * 128:(kc + 1) * 128],
                            identity=ident.ap), reads=[P_sb[pp], ident], writes=[pt_ps2[slot]])
                    ev_eng = "scalar" if (g0 // 4) % 2 == 0 else "vector"
                    ctx.op(ev_eng, lambda e, g0=g0, gn=gn, slot=slot, ev=ev_eng: (
                        e.copy(out=PT_sb.ap[:, g0 * 128:(g0 + gn) * 128],
                               in_=pt_ps2[slot].ap[:, 0:gn * 128]) if ev == "scalar"
                        else e.tensor_copy(PT_sb.ap[:, g0 * 128:(g0 + gn) * 128],
                                           pt_ps2[slot].ap[:, 0:gn * 128])),
                        reads=[pt_ps2[slot]], writes=[PT_sb])
                avsl = it % 2
                def mmav(e, kcn=kcn, h=h, avsl=avsl):
                    for kc in range(kcn):
                        inst = e.matmul(out=av_ps[avsl].ap[:, 0:128],
                                        lhsT=V_res.ap[:, kc, h * 128:(h + 1) * 128],
                                        rhs=PT_sb.ap[:, kc * 128:(kc + 1) * 128],
                                        start=(kc == 0), stop=(kc == kcn - 1))
                    return inst
                ctx.op("tensor", mmav, reads=[V_res, PT_sb], writes=[av_ps[avsl]])
                ctx.op("vector", lambda e, h=h, qsl=qsl, avsl=avsl: e.tensor_tensor(
                    out=gcol[qsl].ap[:, h, :], in0=av_ps[avsl].ap[:, 0:128],
                    in1=gt[qsl].ap[:, h, :], op=ALU.mult),
                    reads=[av_ps[avsl], gt[qsl]], writes=[gcol[qsl]])
                if h == 15:
                    ctx.op("sync", lambda e, qi=qi, qsl=qsl: e.dma_start(
                        out=gd[:, s, qi, :, :], in_=gcol[qsl].ap),
                        reads=[gcol[qsl]], writes=[io["gated_d"]], dma=True)

            for qi in range(nqi):
                qsl = qi % 2
                ctx.op("sync", lambda e, qi=qi, qsl=qsl: e.dma_start(
                    out=qt[qsl].ap, in_=qdl[:, s, qi, :, :]),
                    reads=[io["q_d"]], writes=[qt[qsl]], dma=True)
                ctx.op("sync", lambda e, qi=qi, qsl=qsl: e.dma_start(
                    out=gt[qsl].ap, in_=gdl[:, s, qi, :, :]),
                    reads=[io["gate_d"]], writes=[gt[qsl]], dma=True)
                for h in range(16):
                    it = qi * 16 + h
                    softmax_front(it)
                    if it > 0:
                        attend_back(it - 1)
                if qi == nqi - 1:
                    attend_back(qi * 16 + 15)


def emit_ffn(ctx, io):
    ctx.phase = "f"
    with ctx.scope():
        _emit_ffn_body(ctx, io)


def _emit_ffn_body(ctx, io):
    nc = ctx.nc
    xn_tb = ctx.sb("f_xn", [128, 16, 1024], BF16)
    wfc_sb = ctx.sb("f_wfc", [128, 16, 1024], BF16)
    wpj_sb = ctx.sb("f_wpj", [128, 8, 2048], BF16)
    hsq = ctx.sb("f_hsq", [128, 8, 1024], BF16)
    r_sb = [ctx.sb(f"f_r{i}", [128, 512], BF16) for i in range(2)]
    acc = ctx.sb("f_acc", [128, 16, 1024])
    g_sb = [ctx.sb(f"f_g{i}", [128, 1024], BF16) for i in range(2)]
    h_ps = [ctx.ps(f"f_h{i}", [128, 512]) for i in range(2)]
    o_ps = [ctx.ps(f"f_o{i}", [128, 512]) for i in range(2)]

    xnd = io["xn_f"].ap.rearrange("(dc p) t -> p dc t", p=128)
    wfcd = io["cc_out_fc"].ap.rearrange("(dc p) o -> p dc o", p=128)
    wpjd = io["cc_out_pj"].ap.rearrange("(c p) e -> p c e", p=128)
    gdd = io["g_d"].ap.rearrange("(ec p) t -> p ec t", p=128)

    for tb in range(4):
        ctx.op("sync", lambda e, tb=tb: e.dma_start(
            out=xn_tb.ap, in_=xnd[:, :, tb * 1024:(tb + 1) * 1024]),
            reads=[io["xn_f"]], writes=[xn_tb], dma=True)
        for hc in range(8):
            ctx.op("sync", lambda e, hc=hc: e.dma_start(
                out=wfc_sb.ap, in_=wfcd[:, :, hc * 1024:(hc + 1) * 1024]),
                reads=[io["cc_out_fc"]], writes=[wfc_sb], dma=True)
            ctx.op("sync", lambda e, hc=hc: e.dma_start(
                out=wpj_sb.ap, in_=wpjd[:, hc * 8:(hc + 1) * 8, :]),
                reads=[io["cc_out_pj"]], writes=[wpj_sb], dma=True)
            for hoc in range(8):
                for ng in range(2):
                    psl = (hoc * 2 + ng) % 2
                    def mmh(e, hoc=hoc, ng=ng, psl=psl):
                        for dc in range(16):
                            inst = e.matmul(out=h_ps[psl].ap,
                                            lhsT=wfc_sb.ap[:, dc, hoc * 128:(hoc + 1) * 128],
                                            rhs=xn_tb.ap[:, dc, ng * 512:(ng + 1) * 512],
                                            start=(dc == 0), stop=(dc == 15))
                        return inst
                    ctx.op("tensor", mmh, reads=[wfc_sb, xn_tb], writes=[h_ps[psl]])
                    rsl = (hoc * 2 + ng) % 2
                    ctx.op("scalar", lambda e, psl=psl, rsl=rsl: e.activation(
                        out=r_sb[rsl].ap, in_=h_ps[psl].ap, func=AF.Relu),
                        reads=[h_ps[psl]], writes=[r_sb[rsl]])
                    ctx.op("vector", lambda e, hoc=hoc, ng=ng, rsl=rsl: e.tensor_tensor(
                        out=hsq.ap[:, hoc, ng * 512:(ng + 1) * 512], in0=r_sb[rsl].ap,
                        in1=r_sb[rsl].ap, op=ALU.mult), reads=[r_sb[rsl]], writes=[hsq])
            for ec in range(16):
                for ng in range(2):
                    osl = (ec * 2 + ng) % 2
                    def mmo(e, ec=ec, ng=ng, osl=osl):
                        for hsc in range(8):
                            inst = e.matmul(out=o_ps[osl].ap,
                                            lhsT=wpj_sb.ap[:, hsc, ec * 128:(ec + 1) * 128],
                                            rhs=hsq.ap[:, hsc, ng * 512:(ng + 1) * 512],
                                            start=(hsc == 0), stop=(hsc == 7))
                        return inst
                    ctx.op("tensor", mmo, reads=[wpj_sb, hsq], writes=[o_ps[osl]])
                    if hc == 0:
                        ctx.op("vector", lambda e, ec=ec, ng=ng, osl=osl: e.tensor_copy(
                            acc.ap[:, ec, ng * 512:(ng + 1) * 512], o_ps[osl].ap),
                            reads=[o_ps[osl]], writes=[acc])
                    else:
                        ctx.op("vector", lambda e, ec=ec, ng=ng, osl=osl: e.tensor_tensor(
                            out=acc.ap[:, ec, ng * 512:(ng + 1) * 512],
                            in0=acc.ap[:, ec, ng * 512:(ng + 1) * 512],
                            in1=o_ps[osl].ap, op=ALU.add),
                            reads=[acc, o_ps[osl]], writes=[acc])
        for ec in range(16):
            gsl = ec % 2
            ctx.op("scalar", lambda e, ec=ec, gsl=gsl: e.activation(
                out=g_sb[gsl].ap, in_=acc.ap[:, ec, :], func=AF.Silu),
                reads=[acc], writes=[g_sb[gsl]])
            ctx.op("sync", lambda e, ec=ec, tb=tb, gsl=gsl: e.dma_start(
                out=gdd[:, ec, tb * 1024:(tb + 1) * 1024], in_=g_sb[gsl].ap),
                reads=[g_sb[gsl]], writes=[io["g_d"]], dma=True)



def emit_comb(ctx, io, ident):
    ctx.phase = "c"
    with ctx.scope():
        _emit_comb_body(ctx, io, ident)


def _emit_comb_body(ctx, io, ident):
    nc = ctx.nc
    ow_full = ctx.sb("c_ow", [128, 16, D], BF16)
    with ctx.scope():
        wnatb = [ctx.sb(f"c_wnb{i}", [128, 2048], BF16) for i in range(2)]
        tp_ps = [ctx.ps(f"c_tp{i}", [128, 1024], BF16) for i in range(2)]
        build_wT(ctx, io["o_w"], ow_full, ident, wnatb, tp_ps, list(range(16)))
    g_tb = ctx.sb("c_g", [128, 16, 1024], BF16)
    gated_t = [ctx.sb(f"c_gt{i}", [128, 16, 128], BF16) for i in range(2)]
    sg = ctx.sb("c_sg", [128, 16, 128])
    xs = [ctx.sb(f"c_x{i}", [128, D]) for i in range(2)]
    out_sb = [ctx.sb(f"c_out{i}", [128, D]) for i in range(2)]
    s_ps = [ctx.ps(f"c_s{i}", [128, 512]) for i in range(2)]
    t_ps = [ctx.ps(f"c_t{i}", [128, 2048]) for i in range(1)]

    gdd = io["g_d"].ap.rearrange("(ec p) t -> p ec t", p=128)
    gtd = io["gated_d"].ap.rearrange("s qi p hh q -> p s qi hh q")
    x_t = io["x"].ap.rearrange("(t p) e -> p t e", p=128)
    y_t = io["y"].ap.rearrange("(t p) e -> p t e", p=128)

    if DBG_COMB == "ow":
        return
    for s in range(2):
        for tb in range(2):
            ctx.op("sync", lambda e, s=s, tb=tb: e.dma_start(
                out=g_tb.ap, in_=gdd[:, :, s * 2048 + tb * 1024:s * 2048 + (tb + 1) * 1024]),
                reads=[io["g_d"]], writes=[g_tb], dma=True)
            for qq in range(8):
                qi = tb * 8 + qq
                ti = s * 16 + qi
                sl = qq % 2
                ctx.op("sync", lambda e, s=s, qi=qi, sl=sl: e.dma_start(
                    out=gated_t[sl].ap, in_=gtd[:, s, qi, :, :]),
                    reads=[io["gated_d"]], writes=[gated_t[sl]], dma=True)
                ctx.op("sync", lambda e, ti=ti, sl=sl: e.dma_start(
                    out=xs[sl].ap, in_=x_t[:, ti, :]),
                    reads=[io["x"]], writes=[xs[sl]], dma=True)
                for ec in range(16):
                    osl = ec % 2
                    def mmo(e, ec=ec, sl=sl, osl=osl):
                        for dc in range(16):
                            inst = e.matmul(out=s_ps[osl].ap[:, 0:128],
                                            lhsT=ow_full.ap[:, dc, ec * 128:(ec + 1) * 128],
                                            rhs=gated_t[sl].ap[:, dc, :],
                                            start=(dc == 0), stop=(dc == 15))
                        return inst
                    ctx.op("tensor", mmo, reads=[ow_full, gated_t[sl]], writes=[s_ps[osl]])
                    ctx.op("vector", lambda e, ec=ec, qi=qi, osl=osl: e.tensor_tensor(
                        out=sg.ap[:, ec, :], in0=s_ps[osl].ap[:, 0:128],
                        in1=g_tb.ap[:, ec, (qi % 8) * 128:((qi % 8) + 1) * 128], op=ALU.mult),
                        reads=[s_ps[osl], g_tb], writes=[sg])
                if DBG_COMB == "mm":
                    ctx.op("vector", lambda e, sl=sl: e.tensor_copy(
                        out_sb[sl].ap[:, 0:2048],
                        sg.ap.rearrange("p a b -> p (a b)")), reads=[sg], writes=[out_sb[sl]])
                else:
                    def mtp(e):
                        for ec in range(16):
                            inst = e.transpose(out=t_ps[0].ap[:, ec * 128:(ec + 1) * 128],
                                               in_=sg.ap[:, ec, :], identity=io["identf"].ap)
                        return inst
                    ctx.op("tensor", mtp, reads=[sg, io["identf"]], writes=[t_ps[0]])
                    ctx.op("vector", lambda e, sl=sl: e.tensor_tensor(
                        out=out_sb[sl].ap, in0=xs[sl].ap, in1=t_ps[0].ap, op=ALU.add),
                        reads=[xs[sl], t_ps[0]], writes=[out_sb[sl]])
                ctx.op("sync", lambda e, ti=ti, sl=sl: e.dma_start(
                    out=y_t[:, ti, :], in_=out_sb[sl].ap),
                    reads=[out_sb[sl]], writes=[io["y"]], dma=True)



def finish(ctx):
    nc = ctx.nc
    for name, cnt in ctx.counts.items():
        if cnt > 0:
            nc.sync.wait_ge(ctx.sems[name], cnt)


def build_attn_nc():
    nc = bass.Bass("TRN2", target_bir_lowering=False)
    io = {}
    io["x"] = Buf(nc.dram_tensor("x", [4096, 2048], F32, kind="ExternalInput").ap())
    for n in ("q_w", "k_w", "v_w"):
        io[n] = Buf(nc.dram_tensor(n, [2048, 2048], F32, kind="ExternalInput").ap())
    io["gate_w"] = Buf(nc.dram_tensor("gate_w", [128, 128], F32, kind="ExternalInput").ap())
    io["gate_b"] = Buf(nc.dram_tensor("gate_b", [128, 1], F32, kind="ExternalInput").ap())
    io["identb"] = Buf(nc.dram_tensor("identb", [128, 128], BF16, kind="ExternalInput").ap())
    io["maskc"] = Buf(nc.dram_tensor("maskc", [128, 128], F32, kind="ExternalInput").ap())
    io["xn_f"] = Buf(nc.dram_tensor("xn_f", [2048, 4096], BF16, kind="ExternalOutput").ap())
    io["xn_t"] = Buf(nc.dram_tensor("xn_t", [32, 128, 16, 128], BF16, kind="Internal").ap())
    io["q_d"] = Buf(nc.dram_tensor("q_d", [2, 16, 128, 16, 128], BF16, kind="Internal").ap())
    io["gated_d"] = Buf(nc.dram_tensor("gated_d", [2, 16, 128, 16, 128], BF16,
                                       kind="ExternalOutput").ap())
    io["gate_d"] = Buf(nc.dram_tensor("gate_d", [2, 16, 128, 16, 128], BF16,
                                      kind="Internal").ap())
    with ExitStack() as stack:
        ctx = Ctx(nc, stack)
        ctx.phase = "c0"
        ident = load_const(ctx, io["identb"], "s_id", [128, 128], BF16)
        mk_sb = load_const(ctx, io["maskc"], "s_mk", [128, 128], F32)
        gbraw = load_const(ctx, io["gate_b"], "s_gbr", [128, 1], F32)
        gb_sb = ctx.sb("s_gb", [128, 1])
        ctx.op("vector", lambda e: e.tensor_scalar(
            out=gb_sb.ap, in0=gbraw.ap, scalar1=0.5, scalar2=None, op0=ALU.mult),
            reads=[gbraw], writes=[gb_sb])
        gw_sb = ctx.sb("s_gw", [128, 128], BF16)
        with ctx.scope():
            gwn = load_const(ctx, io["gate_w"], "s_gwn", [128, 128], F32)
            gwnb = ctx.sb("s_gwnb", [128, 128], BF16)
            tp0 = ctx.ps("s_tp0", [128, 1024], BF16)
            ctx.op("vector", lambda e: e.tensor_copy(gwnb.ap, gwn.ap), reads=[gwn], writes=[gwnb])
            ctx.op("tensor", lambda e: e.transpose(out=tp0.ap[:, 0:128], in_=gwnb.ap,
                                                   identity=ident.ap),
                   reads=[gwnb, ident], writes=[tp0])
            ctx.op("scalar", lambda e: e.copy(out=gw_sb.ap, in_=tp0.ap[:, 0:128]),
                   reads=[tp0], writes=[gw_sb])

        emit_ln(ctx, io, ident)
        if DBG_STAGE != "ln":
            nseq = 2 if DBG_STAGE == "full" else 1
            for s in range(nseq):
                emit_attn_seq(ctx, io, s, ident, mk_sb, gw_sb, gb_sb)
        finish(ctx)
    return nc


def build_quant_nc():
    nc = bass.Bass("TRN2", target_bir_lowering=False)
    io = {}
    io["wfc_sl"] = Buf(nc.dram_tensor("wfc_sl", [8192, 256], F32, kind="ExternalInput").ap())
    io["wpj_sl"] = Buf(nc.dram_tensor("wpj_sl", [2048, 1024], F32, kind="ExternalInput").ap())
    io["identb"] = Buf(nc.dram_tensor("identb", [128, 128], BF16, kind="ExternalInput").ap())
    io["cc_in_fc"] = Buf(nc.dram_tensor("cc_in_fc", [256, 8192], BF16, kind="Internal").ap())
    io["cc_in_pj"] = Buf(nc.dram_tensor("cc_in_pj", [1024, 2048], BF16, kind="Internal").ap())
    io["cc_out_fc"] = Buf(nc.dram_tensor("cc_out_fc", [2048, 8192], BF16, kind="Internal",
                                         addr_space="Shared").ap())
    io["cc_out_pj"] = Buf(nc.dram_tensor("cc_out_pj", [8192, 2048], BF16, kind="Internal",
                                         addr_space="Shared").ap())
    wfcT = Buf(nc.dram_tensor("wfcT", [2048, 8192], BF16, kind="ExternalOutput").ap())
    wpjT = Buf(nc.dram_tensor("wpjT", [8192, 2048], BF16, kind="ExternalOutput").ap())
    with ExitStack() as stack:
        ctx = Ctx(nc, stack)
        ctx.phase = "c0"
        ident = load_const(ctx, io["identb"], "s_id", [128, 128], BF16)
        emit_quant(ctx, io, ident)
        ctx.phase = "qc"
        cp = ctx.sb("qc_cp", [128, 8192], BF16)
        for srcb, dstb, rows, cols in ((io["cc_out_fc"], wfcT, 2048, 8192),
                                       (io["cc_out_pj"], wpjT, 8192, 2048)):
            sv = srcb.ap.rearrange("(t p) o -> p t o", p=128)
            dv = dstb.ap.rearrange("(t p) o -> p t o", p=128)
            for t in range(rows // 128):
                ctx.op("sync", lambda e, t=t, sv=sv, cols=cols: e.dma_start(
                    out=cp.ap[:, 0:cols], in_=sv[:, t, :]),
                    reads=[srcb], writes=[cp], dma=True)
                ctx.op("sync", lambda e, t=t, dv=dv, cols=cols: e.dma_start(
                    out=dv[:, t, :], in_=cp.ap[:, 0:cols]),
                    reads=[cp], writes=[dstb], dma=True)
        finish(ctx)
    return nc


def build_ffn_nc():
    nc = bass.Bass("TRN2", target_bir_lowering=False)
    io = {}
    io["xn_f"] = Buf(nc.dram_tensor("xn_f", [2048, 4096], BF16, kind="ExternalInput").ap())
    io["cc_out_fc"] = Buf(nc.dram_tensor("wfcT", [2048, 8192], BF16, kind="ExternalInput").ap())
    io["cc_out_pj"] = Buf(nc.dram_tensor("wpjT", [8192, 2048], BF16, kind="ExternalInput").ap())
    io["g_d"] = Buf(nc.dram_tensor("g_d", [2048, 4096], BF16, kind="ExternalOutput").ap())
    with ExitStack() as stack:
        ctx = Ctx(nc, stack)
        emit_ffn(ctx, io)
        finish(ctx)
    return nc


def build_comb_nc():
    nc = bass.Bass("TRN2", target_bir_lowering=False)
    io = {}
    io["x"] = Buf(nc.dram_tensor("x", [4096, 2048], F32, kind="ExternalInput").ap())
    io["o_w"] = Buf(nc.dram_tensor("o_w", [2048, 2048], F32, kind="ExternalInput").ap())
    io["gated_d"] = Buf(nc.dram_tensor("gated_d", [2, 16, 128, 16, 128], BF16,
                                       kind="ExternalInput").ap())
    io["g_d"] = Buf(nc.dram_tensor("g_d", [2048, 4096], BF16, kind="ExternalInput").ap())
    io["identb"] = Buf(nc.dram_tensor("identb", [128, 128], BF16, kind="ExternalInput").ap())
    io["identf"] = Buf(nc.dram_tensor("identf", [128, 128], F32, kind="ExternalInput").ap())
    io["y"] = Buf(nc.dram_tensor("y", [4096, 2048], F32, kind="ExternalOutput").ap())
    with ExitStack() as stack:
        ctx = Ctx(nc, stack)
        ctx.phase = "c0"
        ident = load_const(ctx, io["identb"], "s_id", [128, 128], BF16)
        identf = load_const(ctx, io["identf"], "s_idf", [128, 128], F32)
        io["identf"] = identf
        emit_comb(ctx, io, ident)
        finish(ctx)
    return nc


def build_fused_nc():
    nc = bass.Bass("TRN2", target_bir_lowering=False)
    io = {}
    io["x"] = Buf(nc.dram_tensor("x", [4096, 2048], F32, kind="ExternalInput").ap())
    for n in ("q_w", "k_w", "v_w", "o_w"):
        io[n] = Buf(nc.dram_tensor(n, [2048, 2048], F32, kind="ExternalInput").ap())
    io["gate_w"] = Buf(nc.dram_tensor("gate_w", [128, 128], F32, kind="ExternalInput").ap())
    io["gate_b"] = Buf(nc.dram_tensor("gate_b", [128, 1], F32, kind="ExternalInput").ap())
    io["wfc_sl"] = Buf(nc.dram_tensor("wfc_sl", [8192, 256], F32, kind="ExternalInput").ap())
    io["wpj_sl"] = Buf(nc.dram_tensor("wpj_sl", [2048, 1024], F32, kind="ExternalInput").ap())
    io["identb"] = Buf(nc.dram_tensor("identb", [128, 128], BF16, kind="ExternalInput").ap())
    io["identf"] = Buf(nc.dram_tensor("identf", [128, 128], F32, kind="ExternalInput").ap())
    io["maskc"] = Buf(nc.dram_tensor("maskc", [128, 128], F32, kind="ExternalInput").ap())
    io["xn_f"] = Buf(nc.dram_tensor("xn_f", [2048, 4096], BF16, kind="Internal").ap())
    io["xn_t"] = Buf(nc.dram_tensor("xn_t", [32, 128, 16, 128], BF16, kind="Internal").ap())
    io["q_d"] = Buf(nc.dram_tensor("q_d", [2, 16, 128, 16, 128], BF16, kind="Internal").ap())
    io["gated_d"] = Buf(nc.dram_tensor("gated_d", [2, 16, 128, 16, 128], BF16,
                                       kind="Internal").ap())
    io["gate_d"] = Buf(nc.dram_tensor("gate_d", [2, 16, 128, 16, 128], BF16,
                                      kind="Internal").ap())
    io["g_d"] = Buf(nc.dram_tensor("g_d", [2048, 4096], BF16, kind="Internal").ap())
    io["cc_in_fc"] = Buf(nc.dram_tensor("cc_in_fc", [256, 8192], BF16, kind="Internal").ap())
    io["cc_in_pj"] = Buf(nc.dram_tensor("cc_in_pj", [1024, 2048], BF16, kind="Internal").ap())
    io["cc_out_fc"] = Buf(nc.dram_tensor("cc_out_fc", [2048, 8192], BF16, kind="Internal",
                                         addr_space="Shared").ap())
    io["cc_out_pj"] = Buf(nc.dram_tensor("cc_out_pj", [8192, 2048], BF16, kind="Internal",
                                         addr_space="Shared").ap())
    io["y"] = Buf(nc.dram_tensor("y", [4096, 2048], F32, kind="ExternalOutput").ap())
    with ExitStack() as stack:
        ctx = Ctx(nc, stack)
        ctx.phase = "c0"
        ident = load_const(ctx, io["identb"], "s_id", [128, 128], BF16)
        mk_sb = load_const(ctx, io["maskc"], "s_mk", [128, 128], F32)
        identf = load_const(ctx, io["identf"], "s_idf", [128, 128], F32)
        io["identf"] = identf
        gbraw = load_const(ctx, io["gate_b"], "s_gbr", [128, 1], F32)
        gb_sb = ctx.sb("s_gb", [128, 1])
        ctx.op("vector", lambda e: e.tensor_scalar(
            out=gb_sb.ap, in0=gbraw.ap, scalar1=0.5, scalar2=None, op0=ALU.mult),
            reads=[gbraw], writes=[gb_sb])
        gw_sb = ctx.sb("s_gw", [128, 128], BF16)
        with ctx.scope():
            gwn = load_const(ctx, io["gate_w"], "s_gwn", [128, 128], F32)
            gwnb = ctx.sb("s_gwnb", [128, 128], BF16)
            tp0 = ctx.ps("s_tp0", [128, 1024], BF16)
            ctx.op("vector", lambda e: e.tensor_copy(gwnb.ap, gwn.ap), reads=[gwn], writes=[gwnb])
            ctx.op("tensor", lambda e: e.transpose(out=tp0.ap[:, 0:128], in_=gwnb.ap,
                                                   identity=ident.ap),
                   reads=[gwnb, ident], writes=[tp0])
            ctx.op("scalar", lambda e: e.copy(out=gw_sb.ap, in_=tp0.ap[:, 0:128]),
                   reads=[tp0], writes=[gw_sb])

        emit_quant(ctx, io, ident)
        emit_ln(ctx, io, ident)
        for s in range(2):
            emit_attn_seq(ctx, io, s, ident, mk_sb, gw_sb, gb_sb)
        emit_ffn(ctx, io)
        emit_comb(ctx, io, ident)
        finish(ctx)
    return nc



def kernel(L0_attn_x, L0_attn_q_w, L0_attn_k_w, L0_attn_v_w, L0_attn_o_w,
           L0_attn_gate_w, L0_attn_gate_b, L0_ffn_w_fc, L0_ffn_w_proj):
    x = np.ascontiguousarray(np.asarray(L0_attn_x, dtype=np.float32))
    q_w = np.ascontiguousarray(np.asarray(L0_attn_q_w, dtype=np.float32))
    k_w = np.ascontiguousarray(np.asarray(L0_attn_k_w, dtype=np.float32))
    v_w = np.ascontiguousarray(np.asarray(L0_attn_v_w, dtype=np.float32))
    o_w = np.ascontiguousarray(np.asarray(L0_attn_o_w, dtype=np.float32))
    gate_w = np.ascontiguousarray(np.asarray(L0_attn_gate_w, dtype=np.float32))
    gate_b = np.ascontiguousarray(np.asarray(L0_attn_gate_b, dtype=np.float32)).reshape(128, 1)
    w_fc = np.ascontiguousarray(np.asarray(L0_ffn_w_fc, dtype=np.float32))
    w_proj = np.ascontiguousarray(np.asarray(L0_ffn_w_proj, dtype=np.float32))

    identb = np.eye(128, dtype=np.float32).astype(ml_dtypes.bfloat16)
    identf = np.eye(128, dtype=np.float32)
    maskc = np.where(np.tril(np.ones((128, 128), dtype=bool)), 0.0, -1e30).astype(np.float32)

    cores = list(range(NCORE))
    xs = [np.ascontiguousarray(x[2 * c:2 * c + 2].reshape(4096, 2048)) for c in cores]
    wfc_sl = [np.ascontiguousarray(w_fc[:, 256 * c:256 * (c + 1)]) for c in cores]
    wpj_sl = [np.ascontiguousarray(w_proj[:, 1024 * c:1024 * (c + 1)]) for c in cores]

    nc = build_fused_nc()
    res = run_bass_kernel_spmd(
        nc, [{"x": xs[c], "q_w": q_w, "k_w": k_w, "v_w": v_w, "o_w": o_w,
              "gate_w": gate_w, "gate_b": gate_b,
              "wfc_sl": wfc_sl[c], "wpj_sl": wpj_sl[c],
              "identb": identb, "identf": identf, "maskc": maskc} for c in cores],
        core_ids=cores)
    ys = [np.asarray(res.results[c]["y"]) for c in cores]
    out = np.concatenate([y.reshape(2, 2048, 2048) for y in ys], axis=0)
    return out.astype(np.float32)
